# version 22
# baseline (speedup 1.0000x reference)
import math
from contextlib import ExitStack

import numpy as np

import concourse.bass as bass
import concourse.mybir as mybir
from concourse.bass_utils import run_bass_kernel_spmd

F32 = mybir.dt.float32
BF16 = mybir.dt.bfloat16
AF = mybir.ActivationFunctionType
ALU = mybir.AluOpType
AX = mybir.AxisListType.X

LN1E4 = math.log(10000.0)
LN1E4_HI = float(np.float32(LN1E4))
LN1E4_LO = float(LN1E4 - LN1E4_HI)
TWO_PI = 2.0 * math.pi
TWO_PI_HI = float(np.float32(TWO_PI))
TWO_PI_LO = float(TWO_PI - TWO_PI_HI)
PI_F = float(np.float32(math.pi))


def cfg_full():
    return dict(S=2, T=2048, D=2048, HF=8192, HD=128, NH=16, NKV=4, KSTEPS=4)


class B:
    ENGINES = ("tensor", "vector", "scalar", "gpsimd", "sync")

    def __init__(self, nc, st):
        self.nc = nc
        self.sem = {e: st.enter_context(nc.semaphore("S_" + e)) for e in self.ENGINES}
        self.cnt = {e: 0 for e in self.ENGINES}
        self.dmaq = {}
        for q in ("sync", "gpsimd", "scalar"):
            self.dmaq[q] = dict(
                sems=[st.enter_context(nc.semaphore(f"D_{q}_{i}")) for i in range(8)],
                tot=[0] * 8, i=0)
        self.bufs = {}
        self.waited = {e: {} for e in self.ENGINES}

    def _wait(self, eng, tok):
        if tok is None:
            return
        sem, val = tok
        d = self.waited[eng]
        k = id(sem)
        if d.get(k, 0) >= val:
            return
        getattr(self.nc, eng).wait_ge(sem, val)
        d[k] = val

    def _dep_waits(self, eng, reads, writes, skip_same=False):
        own = self.sem[eng]
        for bk in reads:
            e = self.bufs.get(bk)
            if e and e["w"] is not None:
                if skip_same and e["w"][0] is own:
                    continue
                self._wait(eng, e["w"])
        for bk in writes:
            e = self.bufs.get(bk)
            if e:
                if e["w"] is not None and not (skip_same and e["w"][0] is own):
                    self._wait(eng, e["w"])
                for t in e["r"]:
                    if skip_same and t[0] is own:
                        continue
                    self._wait(eng, t)

    def _record(self, tok, reads, writes):
        for bk in reads:
            self.bufs.setdefault(bk, dict(w=None, r=[]))["r"].append(tok)
        for bk in writes:
            self.bufs[bk] = dict(w=tok, r=[])

    def op(self, eng, emit, reads=(), writes=()):
        skip = eng == "tensor"
        self._dep_waits(eng, reads, writes, skip_same=skip)
        inst = emit(getattr(self.nc, eng))
        self.cnt[eng] += 1
        inst.then_inc(self.sem[eng], 1)
        tok = (self.sem[eng], self.cnt[eng])
        self._record(tok, reads, writes)
        return tok

    def dma(self, q, dst, src, reads=(), writes=()):
        Q = self.dmaq[q]
        i = Q["i"]
        Q["i"] = (i + 1) % len(Q["sems"])
        sem = Q["sems"][i]
        if Q["tot"][i]:
            self._wait(q, (sem, Q["tot"][i]))
        self._dep_waits(q, reads, writes)
        inst = getattr(self.nc, q).dma_start(dst, src)
        Q["tot"][i] += 16
        inst.then_inc(sem, 16)
        tok = (sem, Q["tot"][i])
        self._record(tok, reads, writes)
        return tok

    def barrier(self):
        toks = [(self.sem[e], self.cnt[e]) for e in self.ENGINES]
        for q, Q in self.dmaq.items():
            for i, sem in enumerate(Q["sems"]):
                if Q["tot"][i]:
                    toks.append((sem, Q["tot"][i]))
        for e in self.ENGINES:
            for t in toks:
                if t[0] is self.sem[e]:
                    continue
                self._wait(e, t)

    def finish(self):
        for q, Q in self.dmaq.items():
            for i, sem in enumerate(Q["sems"]):
                if Q["tot"][i]:
                    self._wait("sync", (sem, Q["tot"][i]))


def build_nc(cfg):
    S, T, D, HF, HD = cfg["S"], cfg["T"], cfg["D"], cfg["HF"], cfg["HD"]
    NH, NKV, KSTEPS = cfg["NH"], cfg["NKV"], cfg["KSTEPS"]
    G = NH // NKV
    KT = D // 128
    TT = T // 128
    KVW = NKV * HD
    assert KVW <= 512
    DC = D // min(512, D)
    CB = min(2048, HF)
    KB = min(1024, T)
    NKB = T // KB
    QSC = 1.0 / math.sqrt(HD)
    NEG_CQC = -1.02 * math.sqrt(KVW * 128.0 / HD)
    NHX = max(NH, 4)

    nc = bass.Bass("TRN2", target_bir_lowering=False)
    ap = lambda h: h.ap()

    x_in = ap(nc.dram_tensor("x", [S, T, D], F32, kind="ExternalInput"))
    y_out = ap(nc.dram_tensor("y", [S, T, D], F32, kind="ExternalOutput"))
    w = {}
    for name, shape in [
        ("qk", [D, NH * HD]), ("kk", [D, KVW]), ("vk", [D, KVW]), ("ok", [NH * HD, D]),
        ("ing", [D]), ("inb", [D]), ("aggw", [D, D]), ("aggb", [D]),
        ("png", [D]), ("pnb", [D]), ("d1w", [D, HF]), ("d1b", [HF]),
        ("d2w", [HF, D]), ("d2b", [D]), ("bcw", [D, D]), ("bcb", [D]),
        ("gw", [D, D]), ("gb", [D]), ("outg", [D]), ("outb", [D]),
    ]:
        w[name] = ap(nc.dram_tensor(name, shape, F32, kind="ExternalInput"))
    xs1 = ap(nc.dram_tensor("xs1", [S, T, D], F32, kind="Internal"))
    xs2 = ap(nc.dram_tensor("xs2", [S, T, D], F32, kind="Internal"))
    GKc = max(1, min(4, 16 // DC))
    NSLAB = (HF // 128) // GKc
    d1bf = ap(nc.dram_tensor("d1bf", [HF // 512, 128, KT * 512], BF16, kind="Internal"))
    d2bf = ap(nc.dram_tensor("d2bf", [NSLAB, 128, GKc * DC * 512], BF16, kind="Internal"))
    qrope = ap(nc.dram_tensor("qrope", [T, D], BF16, kind="Internal"))
    sgd = ap(nc.dram_tensor("sgd", [S, T, D], BF16, kind="Internal"))
    d1v = w["d1w"].rearrange("(a p) n -> p a n", p=128)
    d2v = w["d2w"].rearrange("(a p) n -> p a n", p=128)
    aggv = w["aggw"].rearrange("(a p) n -> p a n", p=128)
    bcv = w["bcw"].rearrange("(a p) n -> p a n", p=128)
    gwv = w["gw"].rearrange("(a p) n -> p a n", p=128)
    qkv_ = w["qk"].rearrange("(a p) n -> p a n", p=128)
    kkv = w["kk"].rearrange("(a p) n -> p a n", p=128)
    vkv = w["vk"].rearrange("(a p) n -> p a n", p=128)
    okv = w["ok"].rearrange("(a p) n -> p a n", p=128)

    with ExitStack() as st:
        b = B(nc, st)
        sb = lambda n, s, d=F32: st.enter_context(nc.sbuf_tensor(n, s, d))
        psb = lambda n, s, d=F32: st.enter_context(nc.psum_tensor(n, s, d))

        pD0 = psb("pD0", [128, 1024])
        pD1 = psb("pD1", [128, 1024])
        pB = psb("pB", [128, 2048], BF16)
        pC0 = psb("pC0", [128, 512])
        pC1 = psb("pC1", [128, 512])

        def PD(c):
            t = (pD0, ("pD0",)) if (c // 2) % 2 == 0 else (pD1, ("pD1",))
            return t[0], (c % 2) * 512, t[1]

        ident = sb("ident", [128, 128], BF16)
        ones1 = sb("ones1", [128, 128], BF16)
        onec = sb("onec", [128, 1])
        b.op("gpsimd", lambda e: e.memset(ident[:, :], 0.0), writes=[("ident",)])
        b.op("gpsimd", lambda e: e.affine_select(
            out=ident[:, :], in_=ident[:, :], compare_op=ALU.not_equal, fill=1.0,
            base=0, pattern=[[-1, 128]], channel_multiplier=1),
            reads=[("ident",)], writes=[("ident",)])
        b.op("gpsimd", lambda e: e.memset(ones1[:, :], 1.0), writes=[("ones1",)])
        b.op("gpsimd", lambda e: e.memset(onec[:, :], 1.0), writes=[("onec",)])

        xt = sb("xt", [128, 3, D])
        st6 = sb("st6", [128, 4, NHX, 6])
        sth2 = sb("sth2", [128, 4, NHX, 2])
        st2 = sb("st2", [128, 4, 2])
        mu = sb("mu", [128, 4])
        rr = sb("rr", [128, 4])
        nmr = sb("nmr", [128, 4])
        n1 = sb("n1", [128, 4])
        n2 = sb("n2", [128, 4])
        msh = sb("msh", [128, 4, NHX])
        xnbf = sb("xnbf", [128, 2, max(D, CB)], BF16)
        xfm = sb("xfm", [128, 2, KT, 128], BF16)
        u0 = sb("u0", [128, max(D, CB)])
        u1 = sb("u1", [128, max(D, CB)])
        wbig = sb("wbig", [128, max(KT, NH), D], BF16)

        def rsqrt_col(v_ap, v_key, sl, eps, rows=128):
            b.op("vector", lambda e: e.tensor_scalar(n1[0:rows, sl:sl + 1], v_ap, 1.0,
                                                     eps, ALU.mult, ALU.add),
                 reads=[v_key], writes=[("n1", sl)])
            b.op("scalar", lambda e: e.activation(n2[0:rows, sl:sl + 1],
                                                  n1[0:rows, sl:sl + 1], AF.Ln),
                 reads=[("n1", sl)], writes=[("n2", sl)])
            b.op("scalar", lambda e: e.activation(rr[0:rows, sl:sl + 1],
                                                  n2[0:rows, sl:sl + 1], AF.Exp,
                                                  scale=-0.5),
                 reads=[("n2", sl)], writes=[("rr", sl)])

        def ln_stats(src_ap, src_key, sl, eps, rows=128, neg_mu_r=False):
            chunk = min(512, D)
            nch = D // chunk
            for c in range(nch):
                b.op("vector", lambda e, c=c: e.bn_stats(
                    st6[0:rows, sl, c, :], src_ap[:, c * chunk:(c + 1) * chunk]),
                    reads=[src_key], writes=[("st6", sl)])
            b.op("vector", lambda e, nch=nch: e.bn_aggr(st2[0:rows, sl, :],
                                                        st6[0:rows, sl, 0:nch, :]),
                 reads=[("st6", sl)], writes=[("st2", sl)])
            b.op("vector", lambda e: e.tensor_copy(mu[0:rows, sl:sl + 1],
                                                   st2[0:rows, sl, 0:1]),
                 reads=[("st2", sl)], writes=[("mu", sl)])
            rsqrt_col(st2[0:rows, sl, 1:2], ("st2", sl), sl, eps, rows)
            if neg_mu_r:
                b.op("vector", lambda e: e.tensor_scalar(
                    nmr[0:rows, sl:sl + 1], mu[0:rows, sl:sl + 1],
                    rr[0:rows, sl:sl + 1], -1.0, ALU.mult, ALU.mult),
                    reads=[("mu", sl), ("rr", sl)], writes=[("nmr", sl)])

        def transpose_tile(src_ap, src_key, tslot, dst_ap=None, dst_key=None):
            for k in range(KT):
                b.op("tensor", lambda e, k=k: e.transpose(
                    pB[:, k * 128:(k + 1) * 128], src_ap[:, k * 128:(k + 1) * 128],
                    ident[:, :]),
                    reads=[src_key, ("ident",)], writes=[("pB",)])
            if dst_ap is None:
                dst_ap = xfm[:, tslot, :, :].rearrange("p a b -> p (a b)")[:, 0:D]
                dst_key = ("xfm", tslot)
            b.op("vector", lambda e, dst_ap=dst_ap, dst_key=dst_key: e.tensor_copy(
                dst_ap, pB[:, 0:D]),
                reads=[("pB",)], writes=[dst_key])

        def ffn(src, src_name, dst, dst_name, prep):
            tag = "_" + dst_name
            with ExitStack() as sc:
                sbl = lambda n, s, d=F32: sc.enter_context(nc.sbuf_tensor(n + tag, s, d))
                win = sbl("win", [128, 3, KT, 512], BF16)
                wbias = sbl("wbias", [1, 3, 512], BF16)
                usig = sbl("usig", [128, 2, D], BF16)
                sA = sbl("sA", [128, S * TT, 2])
                tfm = sbl("tfm", [128, KT, 2], BF16)
                t2fm = sbl("t2fm", [128, HF // 128, 2], BF16)
                png2 = sbl("png2", [2, D], BF16)
                pnb2 = sbl("pnb2", [2, D], BF16)
                brows = sbl("brows", [128, D], BF16)
                gOB = sbl("gOB", [128, D], BF16)
                bOB = sbl("bOB", [128, D], BF16)
                bcB = sbl("bcB", [128, max(D, CB)], BF16)
                vrow = win[0:1, 0, 0:(D // 512), :].rearrange("p a b -> p (a b)")
                th = u0
                ua = u1
                tnb = bcB[0:2, :]


                def bcast_row(dst_ap, dst_key, width, scale=None):
                    for c in range(width // 512):
                        p, off, pk = PD(c)
                        b.op("tensor", lambda e, c=c, p=p, off=off: e.matmul(
                            p[:, off:off + 512], ones1[0:1, :],
                            vrow[:, c * 512:(c + 1) * 512], start=True, stop=True),
                            reads=[("win", 0), ("ones1",)], writes=[pk])
                    for c in range(width // 512):
                        p, off, pk = PD(c)
                        if scale is None:
                            b.op("vector", lambda e, c=c, p=p, off=off: e.tensor_copy(
                                dst_ap[:, c * 512:(c + 1) * 512], p[:, off:off + 512]),
                                reads=[pk], writes=[dst_key])
                        else:
                            b.op("vector", lambda e, c=c, p=p, off=off: e.tensor_scalar(
                                dst_ap[:, c * 512:(c + 1) * 512], p[:, off:off + 512],
                                scale, None, ALU.mult),
                                reads=[pk], writes=[dst_key])

                def load_2row(dstt, dst_key, srcv, n):
                    b.dma("gpsimd", dstt[0:1, 0:n], srcv[None, 0:n], writes=[dst_key])
                    b.dma("gpsimd", dstt[1:2, 0:n], srcv[None, 0:n], writes=[dst_key])

                load_2row(png2, ("png2",), w["ing"], D)
                load_2row(pnb2, ("pnb2",), w["inb"], D)
                b.dma("gpsimd", brows[0:1, :], w["d2b"][None, :], writes=[("brows",)])
                b.dma("gpsimd", brows[32:33, :], w["aggb"][None, :], writes=[("brows",)])
                b.dma("gpsimd", brows[64:65, :], w["bcb"][None, :], writes=[("brows",)])
                b.dma("gpsimd", brows[96:97, :], w["gb"][None, :], writes=[("brows",)])

                def bias_mm(p, off, pk, row, cslice, m):
                    tp = (row, 0) if row else None
                    b.op("tensor", lambda e: e.matmul(
                        p[0:m, off:off + 512], ones1[row:row + 1, 0:m],
                        brows[row:row + 1, cslice:cslice + 512],
                        start=False, stop=True, tile_position=tp),
                        reads=[("brows",), ("ones1",)], writes=[pk])
                b.dma("gpsimd", wbig[:, 0:KT, :], gwv[:, :, :], writes=[("wbig",)])

                def tr2(src_ap, src_key, dst_ap, dst_key, ncols):
                    nk = ncols // 128
                    for k in range(nk):
                        b.op("tensor", lambda e, k=k: e.transpose(
                            pB[:, 2 * k:2 * k + 2],
                            src_ap[0:2, k * 128:(k + 1) * 128], ident[0:2, 0:2]),
                            reads=[src_key, ("ident",)], writes=[("pB",)])
                    b.op("vector", lambda e: e.tensor_copy(
                        dst_ap[:, 0:2 * nk], pB[:, 0:2 * nk]),
                        reads=[("pB",)], writes=[dst_key])

                def ln2row(state_ap, state_key, g2, g2k, b2, b2k, eps):
                    ln_stats(state_ap, state_key, 3, eps, rows=2)
                    b.op("vector", lambda e: e.tensor_scalar(
                        ua[0:2, 0:D], state_ap, mu[0:2, 3:4], rr[0:2, 3:4],
                        ALU.subtract, ALU.mult),
                        reads=[state_key, ("mu", 3), ("rr", 3)], writes=[("u1",)])
                    b.op("vector", lambda e: e.tensor_mul(ua[0:2, 0:D], ua[0:2, 0:D], g2),
                         reads=[("u1",), g2k], writes=[("u1",)])
                    b.op("vector", lambda e: e.tensor_tensor(tnb[0:2, 0:D], ua[0:2, 0:D],
                                                             b2, ALU.add),
                         reads=[("u1",), b2k], writes=[("bcB",)])

                def mm_cols(wview, brow_row, kt_total, n_out):
                    for cc in range(n_out // 512):
                        slot = cc % 3
                        b.dma("gpsimd", win[:, slot, 0:KT, :],
                              wview[:, :, cc * 512:(cc + 1) * 512],
                              writes=[("win", slot)])
                        p, off, pk = PD(cc)
                        for kt in range(kt_total):
                            b.op("tensor", lambda e, kt=kt, slot=slot, p=p, off=off: e.matmul(
                                p[0:2, off:off + 512], tfm[:, kt, :],
                                win[:, slot, kt, :],
                                start=(kt == 0), stop=False),
                                reads=[("tfm",), ("win", slot)], writes=[pk])
                        bias_mm(p, off, pk, brow_row, cc * 512, 2)
                        pump(1)

                def stageA1(s, it):
                    n = s * TT + it
                    slot = n % 3
                    xsl = slot
                    tkey = ("xt", xsl)
                    b.dma("sync", xt[:, xsl, :], src[s, it * 128:(it + 1) * 128, :],
                          reads=[(src_name, s, it)], writes=[tkey])
                    ln_stats(xt[:, xsl, :], tkey, slot, 1e-5, neg_mu_r=True)
                    b.op("vector", lambda e, n=n, slot=slot: e.tensor_copy(
                        sA[:, n, 0:1], nmr[:, slot:slot + 1]),
                        reads=[("nmr", slot)], writes=[("sA", n)])
                    b.op("vector", lambda e, n=n, slot=slot: e.tensor_copy(
                        sA[:, n, 1:2], rr[:, slot:slot + 1]),
                        reads=[("rr", slot)], writes=[("sA", n)])

                def stageA2(s, it):
                    n = s * TT + it
                    slot = n % 3
                    xsl = slot
                    fsl = n % 2
                    b.op("scalar", lambda e, slot=slot, xsl=xsl, fsl=fsl: e.activation(
                        usig[:, fsl, 0:D], xt[:, xsl, :], AF.Identity,
                        bias=nmr[:, slot:slot + 1], scale=rr[:, slot:slot + 1]),
                        reads=[("xt", xsl), ("nmr", slot), ("rr", slot)],
                        writes=[("usig", fsl)])
                    transpose_tile(usig[:, fsl, 0:D], ("usig", fsl), 0,
                                   dst_ap=xfm[:, fsl].rearrange("p a b -> p (a b)")[:, 0:D],
                                   dst_key=("xfm", fsl))

                PCs = [(pC0, ("pC0",)), (pC1, ("pC1",))]

                def stageB_chunk(s, it, c, fsl):
                    p, pk = PCs[c % 2]
                    for kt in range(KT):
                        b.op("tensor", lambda e, c=c, kt=kt, p=p, fsl=fsl: e.matmul(
                            p[:, 0:512], xfm[:, fsl, kt, :],
                            wbig[:, kt, c * 512:(c + 1) * 512],
                            start=(kt == 0), stop=False),
                            reads=[("xfm", fsl), ("wbig",)], writes=[pk])
                    bias_mm(p, 0, pk, 96, c * 512, 128)
                    b.op("scalar", lambda e, c=c, p=p, fsl=fsl: e.activation(
                        usig[:, fsl, c * 512:(c + 1) * 512], p[:, 0:512],
                        AF.Exp, scale=-1.0),
                        reads=[pk], writes=[("usig", fsl)])

                def sig_tail(s, it, fsl):
                    b.op("scalar", lambda e, fsl=fsl: e.activation(
                        usig[:, fsl, 0:D], usig[:, fsl, 0:D],
                        AF.Ln, bias=onec[:, 0:1]),
                        reads=[("usig", fsl), ("onec",)], writes=[("usig", fsl)])
                    b.op("scalar", lambda e, fsl=fsl: e.activation(
                        usig[:, fsl, 0:D], usig[:, fsl, 0:D],
                        AF.Exp, scale=-1.0),
                        reads=[("usig", fsl)], writes=[("usig", fsl)])
                    b.dma("gpsimd", sgd[s, it * 128:(it + 1) * 128, :],
                          usig[:, fsl, 0:D],
                          reads=[("usig", fsl)], writes=[("sgd", s, it)])

                def b1_gen():
                    tiles = [(s_, it_) for s_ in range(S) for it_ in range(TT)]
                    stageA1(*tiles[0])
                    yield
                    stageA2(*tiles[0])
                    yield
                    if len(tiles) > 1:
                        stageA1(*tiles[1])
                        yield
                    for n, (s_, it_) in enumerate(tiles):
                        for c in range(DC):
                            stageB_chunk(s_, it_, c, n % 2)
                            yield
                        if n + 2 < len(tiles):
                            stageA1(*tiles[n + 2])
                            yield
                        sig_tail(s_, it_, n % 2)
                        yield
                        if n + 1 < len(tiles):
                            stageA2(*tiles[n + 1])
                            yield

                gen = b1_gen()
                gen_done = [False]

                def pump(k=1):
                    for _ in range(k):
                        if gen_done[0]:
                            return
                        try:
                            next(gen)
                        except StopIteration:
                            gen_done[0] = True
                            return

                for s in range(S):
                    b.dma("sync", ua[s:s + 1, 0:D], src[s, T - 1:T, :],
                          reads=[(src_name, s, TT - 1)], writes=[("u1",)])
                ln_stats(ua[0:2, 0:D], ("u1",), 3, 1e-5, rows=2)
                b.op("vector", lambda e: e.tensor_scalar(ua[0:2, 0:D], ua[0:2, 0:D],
                                                         mu[0:2, 3:4], rr[0:2, 3:4],
                                                         ALU.subtract, ALU.mult),
                     reads=[("u1",), ("mu", 3), ("rr", 3)], writes=[("u1",)])
                ln_stats(ua[0:2, 0:D], ("u1",), 3, 1e-6, rows=2)
                b.op("vector", lambda e: e.tensor_scalar(ua[0:2, 0:D], ua[0:2, 0:D],
                                                         mu[0:2, 3:4], rr[0:2, 3:4],
                                                         ALU.subtract, ALU.mult),
                     reads=[("u1",), ("mu", 3), ("rr", 3)], writes=[("u1",)])
                b.op("vector", lambda e: e.tensor_mul(ua[0:2, 0:D], ua[0:2, 0:D],
                                                      png2[0:2, 0:D]),
                     reads=[("u1",), ("png2",)], writes=[("u1",)])
                b.op("vector", lambda e: e.tensor_tensor(tnb[0:2, 0:D], ua[0:2, 0:D],
                                                         pnb2[0:2, 0:D], ALU.add),
                     reads=[("u1",), ("pnb2",)], writes=[("bcB",)])
                load_2row(png2, ("png2",), w["png"], D)
                load_2row(pnb2, ("pnb2",), w["pnb"], D)
                tr2(tnb, ("bcB",), tfm.rearrange("p a b -> p (a b)"), ("tfm",), D)
                mm_cols(aggv, 32, KT, D)
                for c in range(DC):
                    p, off, pk = PD(c)
                    b.op("vector", lambda e, c=c, p=p, off=off: e.tensor_copy(
                        th[0:2, c * 512:(c + 1) * 512], p[0:2, off:off + 512]),
                        reads=[pk], writes=[("u0",)])

                for ks in range(KSTEPS):
                    ln2row(th[0:2, 0:D], ("u0",), png2[0:2, 0:D], ("png2",),
                           pnb2[0:2, 0:D], ("pnb2",), 1e-6)
                    tr2(tnb, ("bcB",), tfm.rearrange("p a b -> p (a b)"), ("tfm",), D)
                    slot = 0
                    for cb0 in range(0, HF, CB):
                        for cc in range(CB // 512):
                            c0 = cb0 + cc * 512
                            slot = (cb0 // 512 + cc) % 3
                            b.dma("gpsimd", wbias[0:1, slot, 0:512],
                                  w["d1b"][None, c0:c0 + 512], writes=[("wbias", slot)])
                            if prep and ks == 0:
                                b.dma("gpsimd", win[:, slot, 0:KT, :],
                                      d1v[:, :, c0:c0 + 512], writes=[("win", slot)])
                            else:
                                b.dma("sync" if cc % 2 == 0 else "scalar",
                                      win[:, slot, 0:KT, :],
                                      d1bf[c0 // 512].rearrange("p (a n) -> p a n", n=512),
                                      reads=[("d1bf", c0)], writes=[("win", slot)])
                            p, off, pk = PD(cc)
                            for kt in range(KT):
                                b.op("tensor", lambda e, kt=kt, slot=slot, p=p, off=off: e.matmul(
                                    p[0:2, off:off + 512], tfm[:, kt, :],
                                    win[:, slot, kt, :],
                                    start=(kt == 0), stop=False),
                                    reads=[("tfm",), ("win", slot)], writes=[pk])
                            b.op("tensor", lambda e, p=p, off=off, slot=slot: e.matmul(
                                p[0:2, off:off + 512], ones1[0:1, 0:2],
                                wbias[0:1, slot, 0:512],
                                start=False, stop=True),
                                reads=[("wbias", slot)], writes=[pk])
                            if prep and ks == 0:
                                b.dma("scalar",
                                      d1bf[c0 // 512].rearrange("p (a n) -> p a n", n=512),
                                      win[:, slot, 0:KT, :],
                                      reads=[("win", slot)], writes=[("d1bf", c0)])
                            pump(1)
                        for half in range(CB // 1024):
                            p = pD0 if half == 0 else pD1
                            pk = ("pD0",) if half == 0 else ("pD1",)
                            b.op("scalar", lambda e, half=half, p=p: e.activation(
                                tnb[0:2, half * 1024:(half + 1) * 1024],
                                p[0:2, 0:1024], AF.Gelu),
                                reads=[pk], writes=[("bcB",)])
                        tr2(tnb[0:2, 0:CB], ("bcB",),
                            t2fm.rearrange("p a b -> p (a b)")[:, 2 * cb0 // 128:],
                            ("t2fm",), CB)
                        pump(1)
                    nk2 = HF // 128
                    GK = max(1, min(4, 16 // DC))
                    for kt0 in range(0, nk2, GK):
                        slot2 = (kt0 // GK) % 3
                        gsz = min(GK, nk2 - kt0)
                        slab = win[:, slot2, 0:gsz * DC, :].rearrange(
                            "p (a c) n -> p a c n", c=DC)
                        dview_w = w["d2w"][kt0 * 128:(kt0 + gsz) * 128, :].rearrange(
                            "(a p) (c n) -> p a c n", p=128, n=512)
                        dview_b = d2bf[kt0 // GK].rearrange(
                            "p (a c n) -> p a c n", c=DC, n=512)
                        if prep and ks == 0:
                            b.dma("gpsimd", slab, dview_w, writes=[("win", slot2)])
                            b.dma("scalar", dview_b, slab,
                                  reads=[("win", slot2)], writes=[("d2bf", kt0)])
                        else:
                            b.dma("sync" if (kt0 // GK) % 2 == 0 else "scalar", slab,
                                  dview_b,
                                  reads=[("d2bf", kt0)], writes=[("win", slot2)])
                        pump(1)
                        for g in range(gsz):
                            kt = kt0 + g
                            for cc in range(DC):
                                p, off, pk = PD(cc)
                                b.op("tensor", lambda e, kt=kt, g=g, cc=cc, slot2=slot2, p=p, off=off: e.matmul(
                                    p[0:2, off:off + 512], t2fm[:, kt, :],
                                    win[:, slot2, g * DC + cc, :],
                                    start=(kt == 0), stop=False),
                                    reads=[("t2fm",), ("win", slot2)], writes=[pk])
                        pump(1)
                    for cc in range(DC):
                        p, off, pk = PD(cc)
                        bias_mm(p, off, pk, 0, cc * 512, 2)
                    for cc in range(DC):
                        p, off, pk = PD(cc)
                        b.op("vector", lambda e, cc=cc, p=p, off=off: e.tensor_tensor(
                            th[0:2, cc * 512:(cc + 1) * 512],
                            th[0:2, cc * 512:(cc + 1) * 512],
                            p[0:2, off:off + 512], ALU.add),
                            reads=[("u0",), pk], writes=[("u0",)])

                b.op("vector", lambda e: e.tensor_copy(tnb[0:2, 0:D], th[0:2, 0:D]),
                     reads=[("u0",)], writes=[("bcB",)])
                tr2(tnb, ("bcB",), tfm.rearrange("p a b -> p (a b)"), ("tfm",), D)
                mm_cols(bcv, 64, KT, D)
                for c in range(DC):
                    p, off, pk = PD(c)
                    b.op("vector", lambda e, c=c, p=p, off=off: e.tensor_copy(
                        ua[0:2, c * 512:(c + 1) * 512], p[0:2, off:off + 512]),
                        reads=[pk], writes=[("u1",)])
                b.op("vector", lambda e: e.tensor_copy(brows[32:34, 0:D], ua[0:2, 0:D]),
                     reads=[("u1",)], writes=[("brows",)])

                while not gen_done[0]:
                    pump(1)
                b.dma("gpsimd", vrow[:, 0:D], w["outg"][None, :], writes=[("win", 0)])
                bcast_row(gOB, ("gOB",), D, scale=0.5)
                b.dma("gpsimd", vrow[:, 0:D], w["outb"][None, :], writes=[("win", 0)])
                bcast_row(bOB, ("bOB",), D, scale=0.5)

                def b2A(s, it):
                    xsl = it % 3
                    tkey = ("xt", xsl)
                    b.dma("sync", xt[:, xsl, :], src[s, it * 128:(it + 1) * 128, :],
                          reads=[(src_name, s, it)], writes=[tkey])
                    b.dma("scalar", xnbf[:, it % 2, 0:D], sgd[s, it * 128:(it + 1) * 128, :],
                          reads=[("sgd", s, it)], writes=[("xnbf", it % 2)])

                def b2C(s, it):
                    n = s * TT + it
                    slot = it % 3
                    xsl = it % 3
                    ut, uk = (u0, ("u0",)) if n % 2 == 0 else (u1, ("u1",))
                    b.op("scalar", lambda e, n=n, xsl=xsl, ut=ut: e.activation(
                        ut[:, 0:D], xt[:, xsl, :], AF.Identity,
                        bias=sA[:, n, 0:1], scale=sA[:, n, 1:2]),
                        reads=[("xt", xsl), ("sA", n)],
                        writes=[uk])
                    b.op("vector", lambda e, it=it: e.tensor_mul(
                        xnbf[:, it % 2, 0:D], xnbf[:, it % 2, 0:D], bcB[:, 0:D]),
                        reads=[("xnbf", it % 2), ("bcB",)], writes=[("xnbf", it % 2)])
                    b.op("vector", lambda e, it=it, ut=ut: e.tensor_tensor(
                        ut[:, 0:D], ut[:, 0:D], xnbf[:, it % 2, 0:D], ALU.add),
                        reads=[uk, ("xnbf", it % 2)], writes=[uk])
                    ln_stats(ut[:, 0:D], uk, slot, 1e-6, neg_mu_r=True)
                    b.op("scalar", lambda e, slot=slot, ut=ut: e.activation(
                        ut[:, 0:D], ut[:, 0:D], AF.Identity,
                        bias=nmr[:, slot:slot + 1], scale=rr[:, slot:slot + 1]),
                        reads=[uk, ("nmr", slot), ("rr", slot)], writes=[uk])
                    b.op("vector", lambda e, ut=ut: e.tensor_mul(
                        ut[:, 0:D], ut[:, 0:D], gOB[:, 0:D]),
                        reads=[uk, ("gOB",)], writes=[uk])
                    b.op("vector", lambda e, ut=ut: e.tensor_tensor(
                        ut[:, 0:D], ut[:, 0:D], bOB[:, 0:D], ALU.add),
                        reads=[uk, ("bOB",)], writes=[uk])
                    b.op("vector", lambda e, ut=ut, xsl=xsl: e.tensor_tensor(
                        ut[:, 0:D], ut[:, 0:D], xt[:, xsl, :], ALU.add),
                        reads=[uk, ("xt", xsl)], writes=[uk])
                    b.dma("scalar", dst[s, it * 128:(it + 1) * 128, :], ut[:, 0:D],
                          reads=[uk], writes=[(dst_name, s, it)])

                for s_ in range(S):
                    b.dma("gpsimd", vrow[:, 0:D], brows[32 + s_:33 + s_, 0:D],
                          reads=[("brows",)], writes=[("win", 0)])
                    bcast_row(bcB, ("bcB",), D)
                    b2A(s_, 0)
                    b2A(s_, 1)
                    for it_ in range(TT):
                        b2C(s_, it_)
                        if it_ + 2 < TT:
                            b2A(s_, it_ + 2)
            b.barrier()

        def attn(src, src_name, dst, dst_name):
            with ExitStack() as sc:
                sbl = lambda n, s, d=F32: sc.enter_context(nc.sbuf_tensor(n + "_at", s, d))
                cosTM = sbl("cosTM", [128, T], BF16)
                sinTM = sbl("sinTM", [128, T], BF16)
                with ExitStack() as st_init:
                    sbi = lambda n, s, d=F32: st_init.enter_context(nc.sbuf_tensor(n, s, d))
                    trow = sbi("trow", [1, T])
                    irow = sbi("irow", [1, HD])
                    rtmp = sbi("rtmp", [1, HD])
                    wr0 = sbi("wr0", [128, 128])
                    wr1 = sbi("wr1", [128, 128])
                    wr2 = sbi("wr2", [128, 128])
                    wr3 = sbi("wr3", [128, 128])
                    b.op("gpsimd", lambda e: e.iota(trow[:, :], pattern=[[1, T]], base=0,
                                                    channel_multiplier=0,
                                                    allow_small_or_imprecise_dtypes=True),
                         writes=[("trow",)])
                    b.op("gpsimd", lambda e: e.iota(irow[:, :], pattern=[[1, HD]], base=0,
                                                    channel_multiplier=0,
                                                    allow_small_or_imprecise_dtypes=True),
                         writes=[("irow",)])
                    b.op("vector", lambda e: e.tensor_single_scalar(rtmp[:, :], irow[:, :],
                                                                    64.0, ALU.is_ge),
                         reads=[("irow",)], writes=[("rtmp",)])
                    b.op("vector", lambda e: e.scalar_tensor_tensor(irow[:, :], rtmp[:, :], -64.0,
                                                                    irow[:, :], ALU.mult, ALU.add),
                         reads=[("rtmp",), ("irow",)], writes=[("irow",)])
                    b.op("vector", lambda e: e.tensor_scalar(rtmp[:, :], irow[:, :],
                                                             -LN1E4_LO / 64.0, None, ALU.mult),
                         reads=[("irow",)], writes=[("rtmp",)])
                    b.op("vector", lambda e: e.scalar_tensor_tensor(irow[:, :], irow[:, :],
                                                                    -LN1E4_HI / 64.0, rtmp[:, :],
                                                                    ALU.mult, ALU.add),
                         reads=[("irow",), ("rtmp",)], writes=[("irow",)])
                    b.op("scalar", lambda e: e.activation(irow[:, :], irow[:, :], AF.Exp),
                         reads=[("irow",)], writes=[("irow",)])
                    nstages = max(1, int(math.ceil(math.log2(max(2.0, T / TWO_PI)))))
                    for i in range(TT):
                        b.op("tensor", lambda e, i=i: e.matmul(
                            pD0[:, 0:128], trow[:, i * 128:(i + 1) * 128], irow[:, :],
                            start=True, stop=True),
                            reads=[("trow",), ("irow",)], writes=[("pD0",)])
                        b.op("vector", lambda e: e.tensor_copy(wr1[:, :], pD0[:, 0:128]),
                             reads=[("pD0",)], writes=[("wr1",)])
                        b.op("vector", lambda e: e.memset(wr2[:, :], 0.0), writes=[("wr2",)])
                        for j in range(nstages - 1, -1, -1):
                            cj = TWO_PI_HI * float(2 ** j)
                            b.op("vector", lambda e, cj=cj: e.tensor_single_scalar(
                                wr3[:, :], wr1[:, :], cj, ALU.is_ge),
                                reads=[("wr1",)], writes=[("wr3",)])
                            b.op("vector", lambda e, cj=cj: e.scalar_tensor_tensor(
                                wr1[:, :], wr3[:, :], -cj, wr1[:, :], ALU.mult, ALU.add),
                                reads=[("wr3",), ("wr1",)], writes=[("wr1",)])
                            b.op("vector", lambda e, j=j: e.scalar_tensor_tensor(
                                wr2[:, :], wr3[:, :], float(2 ** j), wr2[:, :],
                                ALU.mult, ALU.add),
                                reads=[("wr3",), ("wr2",)], writes=[("wr2",)])
                        b.op("vector", lambda e: e.scalar_tensor_tensor(
                            wr1[:, :], wr2[:, :], -TWO_PI_LO, wr1[:, :], ALU.mult, ALU.add),
                            reads=[("wr2",), ("wr1",)], writes=[("wr1",)])
                        b.op("vector", lambda e: e.tensor_single_scalar(
                            wr3[:, :], wr1[:, :], PI_F, ALU.is_ge),
                            reads=[("wr1",)], writes=[("wr3",)])
                        b.op("vector", lambda e: e.scalar_tensor_tensor(
                            wr0[:, :], wr3[:, :], -TWO_PI_HI, wr1[:, :], ALU.mult, ALU.add),
                            reads=[("wr3",), ("wr1",)], writes=[("wr0",)])
                        b.op("scalar", lambda e, i=i: e.activation(
                            sinTM[:, i * 128:(i + 1) * 128], wr0[:, :], AF.Sin),
                            reads=[("wr0",)], writes=[("sinTM",)])
                        b.op("vector", lambda e, i=i: e.tensor_scalar(
                            sinTM[:, i * 128:i * 128 + 64], sinTM[:, i * 128:i * 128 + 64],
                            -1.0, None, ALU.mult),
                            reads=[("sinTM",)], writes=[("sinTM",)])
                        b.op("vector", lambda e: e.tensor_scalar(
                            wr0[:, :], wr1[:, :], 1.0, PI_F / 2.0, ALU.mult, ALU.add),
                            reads=[("wr1",)], writes=[("wr0",)])
                        b.op("vector", lambda e: e.tensor_single_scalar(
                            wr3[:, :], wr0[:, :], PI_F, ALU.is_ge),
                            reads=[("wr0",)], writes=[("wr3",)])
                        b.op("vector", lambda e: e.scalar_tensor_tensor(
                            wr0[:, :], wr3[:, :], -TWO_PI_HI, wr0[:, :], ALU.mult, ALU.add),
                            reads=[("wr3",), ("wr0",)], writes=[("wr0",)])
                        b.op("scalar", lambda e, i=i: e.activation(
                            cosTM[:, i * 128:(i + 1) * 128], wr0[:, :], AF.Sin),
                            reads=[("wr0",)], writes=[("cosTM",)])
                    b.barrier()

                kvt = sbl("kvt", [128, KT, KVW], BF16)
                qbuf = sbl("qbuf", [128, 2, D], BF16)
                k_fm = sbl("k_fm", [128, NKV, T], BF16)
                v_tm = sbl("v_tm", [128, TT, KVW], BF16)
                cq = sbl("cq", [128, TT, NH], BF16)
                sums = sbl("sums", [128, NH])
                sacc = sbl("sacc", [128, 2, 2])
                rsum = sbl("rsum", [128, NH])
                probs = sbl("probs", [128, 2, max(T, D)], BF16)
                qsb = probs[:, 0, 0:D]
                ksb = probs[:, 1, 0:KVW]
                probsT = sbl("probsT", [128, 2, TT, 128], BF16)
                encT = sbl("encT", [128, NH, 128], BF16)

                def rms_rope(src_sb, src_sb_key, width, ctile, stile,
                             out_ap, out_key, extra_scale, want_cq, it, sl):
                    nch = width // 128
                    for c in range(nch):
                        b.op("vector", lambda e, c=c: e.bn_stats(
                            st6[:, sl, c, :], src_sb[:, c * 128:(c + 1) * 128]),
                            reads=[src_sb_key], writes=[("st6", sl)])
                    for h in range(nch):
                        b.op("vector", lambda e, h=h: e.bn_aggr(
                            sth2[:, sl, h, :], st6[:, sl, h:h + 1, :]),
                            reads=[("st6", sl)], writes=[("sth2", sl)])
                    b.op("vector", lambda e, nch=nch: e.tensor_mul(
                        msh[:, sl, 0:nch], sth2[:, sl, 0:nch, 0], sth2[:, sl, 0:nch, 0]),
                        reads=[("sth2", sl)], writes=[("msh", sl)])
                    b.op("vector", lambda e, nch=nch: e.tensor_tensor(
                        msh[:, sl, 0:nch], msh[:, sl, 0:nch], sth2[:, sl, 0:nch, 1],
                        ALU.add),
                        reads=[("msh", sl), ("sth2", sl)], writes=[("msh", sl)])
                    b.op("vector", lambda e, nch=nch: e.tensor_reduce(
                        st2[:, sl, 0:1], msh[:, sl, 0:nch], AX, ALU.add),
                        reads=[("msh", sl)], writes=[("st2", sl)])
                    b.op("vector", lambda e, nch=nch: e.tensor_scalar(
                        st2[:, sl, 0:1], st2[:, sl, 0:1], 1.0 / nch, None, ALU.mult),
                        reads=[("st2", sl)], writes=[("st2", sl)])
                    rsqrt_col(st2[:, sl, 0:1], ("st2", sl), sl, 1e-6)
                    srcv = src_sb.rearrange("p (a n) -> p a n", n=128)
                    u0v = u0[:, 0:width].rearrange("p (a n) -> p a n", n=128)
                    u1v = u1[:, 0:width].rearrange("p (a n) -> p a n", n=128)
                    cb_ = ctile.unsqueeze(1).broadcast_to([128, nch, 128])
                    sb0 = stile[:, 0:64].unsqueeze(1).broadcast_to([128, nch, 64])
                    sb1 = stile[:, 64:128].unsqueeze(1).broadcast_to([128, nch, 64])
                    b.op("vector", lambda e: e.tensor_tensor(u0v, srcv, cb_, ALU.mult),
                         reads=[src_sb_key, ("cosTM",)], writes=[("u0",)])
                    b.op("vector", lambda e: e.tensor_tensor(
                        u1v[:, :, 0:64], srcv[:, :, 64:128], sb0, ALU.mult),
                        reads=[src_sb_key, ("sinTM",)], writes=[("u1",)])
                    b.op("vector", lambda e: e.tensor_tensor(
                        u1v[:, :, 64:128], srcv[:, :, 0:64], sb1, ALU.mult),
                        reads=[src_sb_key, ("sinTM",)], writes=[("u1",)])
                    b.op("vector", lambda e: e.tensor_tensor(
                        u0[:, 0:width], u0[:, 0:width], u1[:, 0:width], ALU.add),
                        reads=[("u0",), ("u1",)], writes=[("u0",)])
                    b.op("vector", lambda e: e.tensor_scalar(
                        out_ap, u0[:, 0:width], rr[:, sl:sl + 1], extra_scale,
                        ALU.mult, ALU.mult),
                        reads=[("u0",), ("rr", sl)], writes=[out_key])
                    if want_cq:
                        b.op("scalar", lambda e, nch=nch: e.activation(
                            msh[:, sl, 0:nch], msh[:, sl, 0:nch], AF.Ln),
                            reads=[("msh", sl)], writes=[("msh", sl)])
                        b.op("scalar", lambda e, nch=nch: e.activation(
                            msh[:, sl, 0:nch], msh[:, sl, 0:nch], AF.Exp, scale=0.5),
                            reads=[("msh", sl)], writes=[("msh", sl)])
                        b.op("vector", lambda e, nch=nch: e.tensor_scalar(
                            msh[:, sl, 0:nch], msh[:, sl, 0:nch], rr[:, sl:sl + 1],
                            NEG_CQC, ALU.mult, ALU.mult),
                            reads=[("msh", sl), ("rr", sl)], writes=[("msh", sl)])
                        b.op("vector", lambda e, it=it, nch=nch: e.tensor_copy(
                            cq[:, it, 0:nch], msh[:, sl, 0:nch]),
                            reads=[("msh", sl)], writes=[("cq", it)])

                def a_stage1(s, it):
                    slot = it % 2
                    xsl = it % 3
                    tkey = ("xt", xsl)
                    b.dma("sync", xt[:, xsl, :], src[s, it * 128:(it + 1) * 128, :],
                          reads=[(src_name, s, it)], writes=[tkey])
                    ln_stats(xt[:, xsl, :], tkey, slot, 1e-5, neg_mu_r=True)
                    b.op("scalar", lambda e, slot=slot, xsl=xsl: e.activation(
                        xnbf[:, slot, 0:D], xt[:, xsl, :], AF.Identity,
                        bias=nmr[:, slot:slot + 1], scale=rr[:, slot:slot + 1]),
                        reads=[tkey, ("nmr", slot), ("rr", slot)],
                        writes=[("xnbf", slot)])
                    transpose_tile(xnbf[:, slot, 0:D], ("xnbf", slot), slot)

                def a_proj(s, it):
                    slot = it % 2
                    for c in range(DC):
                        p, off, pk = PD(c)
                        for kt in range(KT):
                            b.op("tensor", lambda e, c=c, kt=kt, p=p, off=off, slot=slot: e.matmul(
                                p[:, off:off + 512], xfm[:, slot, kt, :],
                                wbig[:, kt, c * 512:(c + 1) * 512],
                                start=(kt == 0), stop=(kt == KT - 1)),
                                reads=[("xfm", slot), ("wbig",)], writes=[pk])
                    for kt in range(KT):
                        b.op("tensor", lambda e, kt=kt, slot=slot: e.matmul(
                            pC0[:, 0:KVW], xfm[:, slot, kt, :], kvt[:, kt, 0:KVW],
                            start=(kt == 0), stop=(kt == KT - 1)),
                            reads=[("xfm", slot), ("kvt",)], writes=[("pC0",)])

                def a_rope(s, it):
                    slot = it % 2
                    for c in range(DC):
                        p, off, pk = PD(c)
                        b.op("vector", lambda e, c=c, p=p, off=off: e.tensor_copy(
                            qsb[:, c * 512:(c + 1) * 512], p[:, off:off + 512]),
                            reads=[pk], writes=[("probs", 0)])
                    b.op("vector", lambda e: e.tensor_copy(ksb[:, 0:KVW], pC0[:, 0:KVW]),
                         reads=[("pC0",)], writes=[("probs", 1)])
                    ctile = cosTM[:, it * 128:(it + 1) * 128]
                    stile = sinTM[:, it * 128:(it + 1) * 128]
                    rms_rope(qsb, ("probs", 0), D, ctile, stile,
                             qbuf[:, slot, :], ("qbuf", slot), QSC, True, it, slot)
                    for k in range(KT):
                        b.op("tensor", lambda e, k=k, slot=slot: e.transpose(
                            pB[:, k * 128:(k + 1) * 128],
                            qbuf[:, slot, k * 128:(k + 1) * 128], ident[:, :]),
                            reads=[("qbuf", slot), ("ident",)], writes=[("pB",)])
                    b.op("vector", lambda e, slot=slot: e.tensor_copy(
                        qbuf[:, slot, 0:D], pB[:, 0:D]),
                        reads=[("pB",)], writes=[("qbuf", slot)])
                    b.dma("scalar", qrope[it * 128:(it + 1) * 128, :],
                          qbuf[:, slot, :], reads=[("qbuf", slot)],
                          writes=[("qrope", it)])
                    rms_rope(ksb, ("probs", 1), KVW, ctile, stile,
                             xnbf[:, slot, 0:KVW], ("xnbf", slot), 1.0, False, it, slot)
                    for h in range(NKV):
                        b.op("tensor", lambda e, h=h, slot=slot: e.transpose(
                            pB[:, h * 128:(h + 1) * 128],
                            xnbf[:, slot, h * 128:(h + 1) * 128], ident[:, :]),
                            reads=[("xnbf", slot), ("ident",)], writes=[("pB",)])
                    b.op("vector", lambda e, it=it: e.tensor_copy(
                        k_fm[:, 0:NKV, it * 128:(it + 1) * 128],
                        pB[:, 0:NKV * 128].rearrange("p (a b) -> p a b", b=128)),
                        reads=[("pB",)], writes=[("k_fm", it)])

                kf_reads = [("k_fm", i) for i in range(TT)]
                vt_reads = [("v_tm", i) for i in range(TT)]

                def c2_load(it):
                    slot = it % 2
                    b.dma("sync",
                          xfm[:, slot, :, :].rearrange("p a b -> p (a b)")[:, 0:D],
                          qrope[it * 128:(it + 1) * 128, :],
                          reads=[("qrope", it)], writes=[("xfm", slot)])

                def c2_stage1(it, h):
                    slot = it % 2
                    kv = h // G
                    ps_ = h % 2
                    for kb in range(NKB):
                        p = pD0 if (kb + NKB * h) % 2 == 0 else pD1
                        pk = ("pD0",) if (kb + NKB * h) % 2 == 0 else ("pD1",)
                        kch = min(512, KB)
                        for c in range(KB // kch):
                            b.op("tensor", lambda e, h=h, kb=kb, c=c, kv=kv, p=p, kch=kch, slot=slot: e.matmul(
                                p[:, c * kch:(c + 1) * kch], xfm[:, slot, h, :],
                                k_fm[:, kv, kb * KB + c * kch:kb * KB + (c + 1) * kch],
                                start=True, stop=True),
                                reads=[("xfm", slot)] + kf_reads, writes=[pk])
                        b.op("scalar", lambda e, it=it, h=h, kb=kb, p=p, ps_=ps_: e.activation(
                            xnbf[:, ps_, kb * KB:(kb + 1) * KB], p[:, 0:KB],
                            AF.Exp, bias=cq[:, it, h:h + 1], scale=1.0,
                            accum_out=sacc[:, ps_, kb:kb + 1]),
                            reads=[pk, ("cq", it)],
                            writes=[("xnbf", ps_), ("sacc", ps_, kb)])
                def c2_stage2(it, h):
                    kv = h // G
                    ps_ = h % 2
                    if NKB == 2:
                        b.op("vector", lambda e, h=h: e.tensor_tensor(
                            sums[:, h:h + 1], sacc[:, h % 2, 0:1], sacc[:, h % 2, 1:2],
                            ALU.add),
                            reads=[("sacc", h % 2, 0), ("sacc", h % 2, 1)],
                            writes=[("sums", h)])
                    else:
                        b.op("vector", lambda e, h=h: e.tensor_copy(
                            sums[:, h:h + 1], sacc[:, h % 2, 0:1]),
                            reads=[("sacc", h % 2, 0)], writes=[("sums", h)])
                    b.op("vector", lambda e, h=h: e.reciprocal(
                        rsum[:, h:h + 1], sums[:, h:h + 1]),
                        reads=[("sums", h)], writes=[("rsum", h)])
                    b.op("vector", lambda e, h=h, ps_=ps_: e.tensor_scalar(
                        xnbf[:, ps_, 0:T], xnbf[:, ps_, 0:T],
                        rsum[:, h:h + 1], None, ALU.mult),
                        reads=[("xnbf", ps_), ("rsum", h)], writes=[("xnbf", ps_)])
                    for c in range(TT):
                        b.op("tensor", lambda e, c=c, ps_=ps_: e.transpose(
                            pB[:, c * 128:(c + 1) * 128],
                            xnbf[:, ps_, c * 128:(c + 1) * 128], ident[:, :]),
                            reads=[("xnbf", ps_), ("ident",)], writes=[("pB",)])
                    half = T // 2
                    b.op("vector", lambda e, half=half, ps_=ps_: e.tensor_copy(
                        probsT[:, ps_, :, :].rearrange("p a b -> p (a b)")[:, 0:half],
                        pB[:, 0:half]),
                        reads=[("pB",)], writes=[("probsT", ps_)])
                    b.op("scalar", lambda e, half=half, ps_=ps_: e.activation(
                        probsT[:, ps_, :, :].rearrange("p a b -> p (a b)")[:, half:T],
                        pB[:, half:T], AF.Identity),
                        reads=[("pB",)], writes=[("probsT", ps_)])
                def c2_stage2b(it, h):
                    kv = h // G
                    ps_ = h % 2
                    pv, pvk = (pC0, ("pC0",)) if h % 2 == 0 else (pC1, ("pC1",))
                    for kt in range(TT):
                        b.op("tensor", lambda e, kt=kt, kv=kv, pv=pv, ps_=ps_: e.matmul(
                            pv[:, 0:128], v_tm[:, kt, kv * 128:(kv + 1) * 128],
                            probsT[:, ps_, kt, :],
                            start=(kt == 0), stop=(kt == TT - 1)),
                            reads=[("probsT", ps_)] + vt_reads, writes=[pvk])
                    b.op("vector", lambda e, h=h, pv=pv, pvk=pvk: e.tensor_copy(
                        encT[:, h, :], pv[:, 0:128]),
                        reads=[pvk], writes=[("encT", h)])

                enc_reads = [("encT", i) for i in range(NH)]

                for s in range(S):
                    b.dma("gpsimd", wbig[:, 0:KT, :], qkv_[:, :, :], writes=[("wbig",)])
                    b.dma("gpsimd", kvt[:, :, 0:KVW], kkv[:, :, :], writes=[("kvt",)])
                    a_stage1(s, 0)
                    for it in range(TT):
                        a_proj(s, it)
                        if it + 1 < TT:
                            a_stage1(s, it + 1)
                        a_rope(s, it)

                    b.dma("gpsimd", kvt[:, :, 0:KVW], vkv[:, :, :], writes=[("kvt",)])
                    a_stage1(s, 0)
                    for it in range(TT):
                        slot = it % 2
                        pv = pC0 if it % 2 == 0 else pC1
                        pvk = ("pC0",) if it % 2 == 0 else ("pC1",)
                        for kt in range(KT):
                            b.op("tensor", lambda e, kt=kt, slot=slot, pv=pv: e.matmul(
                                pv[:, 0:KVW], xfm[:, slot, kt, :], kvt[:, kt, 0:KVW],
                                start=(kt == 0), stop=(kt == KT - 1)),
                                reads=[("xfm", slot), ("kvt",)], writes=[pvk])
                        if it + 1 < TT:
                            a_stage1(s, it + 1)
                        b.op("vector", lambda e, it=it, pv=pv, pvk=pvk: e.tensor_copy(
                            v_tm[:, it, 0:KVW], pv[:, 0:KVW]),
                            reads=[pvk], writes=[("v_tm", it)])

                    b.dma("gpsimd", wbig[:, 0:NH, :], okv[:, :, :], writes=[("wbig",)])

                    c2_load(0)
                    for it in range(TT):
                        slot = it % 2
                        xsl = it % 3
                        tkey = ("xt", xsl)
                        b.dma("sync", xt[:, xsl, :], src[s, it * 128:(it + 1) * 128, :],
                              reads=[(src_name, s, it)], writes=[tkey])
                        if it + 1 < TT:
                            c2_load(it + 1)
                        for h in range(NH):
                            c2_stage1(it, h)
                            if h > 0:
                                c2_stage2(it, h - 1)
                            if h > 1:
                                c2_stage2b(it, h - 2)
                        c2_stage2(it, NH - 1)
                        c2_stage2b(it, NH - 2)
                        c2_stage2b(it, NH - 1)
                        for c in range(DC):
                            p, off, pk = PD(c)
                            for kt in range(NH):
                                b.op("tensor", lambda e, c=c, kt=kt, p=p, off=off: e.matmul(
                                    p[:, off:off + 512], encT[:, kt, :],
                                    wbig[:, kt, c * 512:(c + 1) * 512],
                                    start=(kt == 0), stop=(kt == NH - 1)),
                                    reads=[("wbig",)] + enc_reads, writes=[pk])
                        for c in range(DC):
                            p, off, pk = PD(c)
                            b.op("vector", lambda e, c=c, xsl=xsl, p=p, off=off: e.tensor_tensor(
                                u1[:, c * 512:(c + 1) * 512], p[:, off:off + 512],
                                xt[:, xsl, c * 512:(c + 1) * 512], ALU.add),
                                reads=[pk, tkey], writes=[("u1",)])
                        b.dma("scalar", dst[s, it * 128:(it + 1) * 128, :], u1[:, 0:D],
                              reads=[("u1",)], writes=[(dst_name, s, it)])
            b.barrier()

        ffn(x_in, "x", xs1, "xs1", prep=True)
        attn(xs1, "xs1", xs2, "xs2")
        ffn(xs2, "xs2", y_out, "y", prep=False)

        b.finish()
    return nc


def kernel(**inputs):
    cfg = cfg_full()
    n_cores = 8
    per = inputs["L0_attn_x"].shape[0] // n_cores
    nc = build_nc(cfg)
    namemap = {
        "qk": "L0_attn_q_kernel", "kk": "L0_attn_k_kernel", "vk": "L0_attn_v_kernel",
        "ok": "L0_attn_out_kernel", "ing": "L0_ffn_in_gamma", "inb": "L0_ffn_in_beta",
        "aggw": "L0_ffn_agg_w", "aggb": "L0_ffn_agg_b", "png": "L0_ffn_pn_gamma",
        "pnb": "L0_ffn_pn_beta", "d1w": "L0_ffn_d1_w", "d1b": "L0_ffn_d1_b",
        "d2w": "L0_ffn_d2_w", "d2b": "L0_ffn_d2_b", "bcw": "L0_ffn_bc_w",
        "bcb": "L0_ffn_bc_b", "gw": "L0_ffn_gate_w", "gb": "L0_ffn_gate_b",
        "outg": "L0_ffn_out_gamma", "outb": "L0_ffn_out_beta",
    }
    x = np.ascontiguousarray(np.asarray(inputs["L0_attn_x"], dtype=np.float32))
    weights = {k: np.ascontiguousarray(np.asarray(inputs[v], dtype=np.float32))
               for k, v in namemap.items()}
    in_maps = []
    for c in range(n_cores):
        m = {"x": x[c * per:(c + 1) * per]}
        m.update(weights)
        in_maps.append(m)
    res = run_bass_kernel_spmd(nc, in_maps, core_ids=list(range(n_cores)))
    return np.concatenate([np.asarray(r["y"]) for r in res.results], axis=0)


# version 23
# speedup vs baseline: 1.0149x; 1.0149x over previous
import math
from contextlib import ExitStack

import numpy as np

import concourse.bass as bass
import concourse.mybir as mybir
from concourse.bass_utils import run_bass_kernel_spmd

F32 = mybir.dt.float32
BF16 = mybir.dt.bfloat16
AF = mybir.ActivationFunctionType
ALU = mybir.AluOpType
AX = mybir.AxisListType.X

LN1E4 = math.log(10000.0)
LN1E4_HI = float(np.float32(LN1E4))
LN1E4_LO = float(LN1E4 - LN1E4_HI)
TWO_PI = 2.0 * math.pi
TWO_PI_HI = float(np.float32(TWO_PI))
TWO_PI_LO = float(TWO_PI - TWO_PI_HI)
PI_F = float(np.float32(math.pi))


def cfg_full():
    return dict(S=2, T=2048, D=2048, HF=8192, HD=128, NH=16, NKV=4, KSTEPS=4)


class B:
    ENGINES = ("tensor", "vector", "scalar", "gpsimd", "sync")

    def __init__(self, nc, st):
        self.nc = nc
        self.sem = {e: st.enter_context(nc.semaphore("S_" + e)) for e in self.ENGINES}
        self.cnt = {e: 0 for e in self.ENGINES}
        self.dmaq = {}
        for q in ("sync", "gpsimd", "scalar"):
            self.dmaq[q] = dict(
                sems=[st.enter_context(nc.semaphore(f"D_{q}_{i}")) for i in range(8)],
                tot=[0] * 8, i=0)
        self.bufs = {}
        self.waited = {e: {} for e in self.ENGINES}

    def _wait(self, eng, tok):
        if tok is None:
            return
        sem, val = tok
        d = self.waited[eng]
        k = id(sem)
        if d.get(k, 0) >= val:
            return
        getattr(self.nc, eng).wait_ge(sem, val)
        d[k] = val

    def _dep_waits(self, eng, reads, writes, skip_same=False):
        own = self.sem[eng]
        for bk in reads:
            e = self.bufs.get(bk)
            if e and e["w"] is not None:
                if skip_same and e["w"][0] is own:
                    continue
                self._wait(eng, e["w"])
        for bk in writes:
            e = self.bufs.get(bk)
            if e:
                if e["w"] is not None and not (skip_same and e["w"][0] is own):
                    self._wait(eng, e["w"])
                for t in e["r"]:
                    if skip_same and t[0] is own:
                        continue
                    self._wait(eng, t)

    def _record(self, tok, reads, writes):
        for bk in reads:
            self.bufs.setdefault(bk, dict(w=None, r=[]))["r"].append(tok)
        for bk in writes:
            self.bufs[bk] = dict(w=tok, r=[])

    def op(self, eng, emit, reads=(), writes=()):
        skip = eng == "tensor"
        self._dep_waits(eng, reads, writes, skip_same=skip)
        inst = emit(getattr(self.nc, eng))
        self.cnt[eng] += 1
        inst.then_inc(self.sem[eng], 1)
        tok = (self.sem[eng], self.cnt[eng])
        self._record(tok, reads, writes)
        return tok

    def dma(self, q, dst, src, reads=(), writes=()):
        Q = self.dmaq[q]
        i = Q["i"]
        Q["i"] = (i + 1) % len(Q["sems"])
        sem = Q["sems"][i]
        if Q["tot"][i]:
            self._wait(q, (sem, Q["tot"][i]))
        self._dep_waits(q, reads, writes)
        inst = getattr(self.nc, q).dma_start(dst, src)
        Q["tot"][i] += 16
        inst.then_inc(sem, 16)
        tok = (sem, Q["tot"][i])
        self._record(tok, reads, writes)
        return tok

    def barrier(self):
        toks = [(self.sem[e], self.cnt[e]) for e in self.ENGINES]
        for q, Q in self.dmaq.items():
            for i, sem in enumerate(Q["sems"]):
                if Q["tot"][i]:
                    toks.append((sem, Q["tot"][i]))
        for e in self.ENGINES:
            for t in toks:
                if t[0] is self.sem[e]:
                    continue
                self._wait(e, t)

    def finish(self):
        for q, Q in self.dmaq.items():
            for i, sem in enumerate(Q["sems"]):
                if Q["tot"][i]:
                    self._wait("sync", (sem, Q["tot"][i]))


def build_nc(cfg):
    S, T, D, HF, HD = cfg["S"], cfg["T"], cfg["D"], cfg["HF"], cfg["HD"]
    NH, NKV, KSTEPS = cfg["NH"], cfg["NKV"], cfg["KSTEPS"]
    G = NH // NKV
    KT = D // 128
    TT = T // 128
    KVW = NKV * HD
    assert KVW <= 512
    DC = D // min(512, D)
    CB = min(2048, HF)
    KB = min(1024, T)
    NKB = T // KB
    QSC = 1.0 / math.sqrt(HD)
    NEG_CQC = -1.02 * math.sqrt(KVW * 128.0 / HD)
    NHX = max(NH, 4)

    nc = bass.Bass("TRN2", target_bir_lowering=False)
    ap = lambda h: h.ap()

    x_in = ap(nc.dram_tensor("x", [S, T, D], F32, kind="ExternalInput"))
    y_out = ap(nc.dram_tensor("y", [S, T, D], F32, kind="ExternalOutput"))
    w = {}
    for name, shape in [
        ("qk", [D, NH * HD]), ("kk", [D, KVW]), ("vk", [D, KVW]), ("ok", [NH * HD, D]),
        ("ing", [D]), ("inb", [D]), ("aggw", [D, D]), ("aggb", [D]),
        ("png", [D]), ("pnb", [D]), ("d1w", [D, HF]), ("d1b", [HF]),
        ("d2w", [HF, D]), ("d2b", [D]), ("bcw", [D, D]), ("bcb", [D]),
        ("gw", [D, D]), ("gb", [D]), ("outg", [D]), ("outb", [D]),
    ]:
        w[name] = ap(nc.dram_tensor(name, shape, F32, kind="ExternalInput"))
    xs1 = ap(nc.dram_tensor("xs1", [S, T, D], F32, kind="Internal"))
    xs2 = ap(nc.dram_tensor("xs2", [S, T, D], F32, kind="Internal"))
    GKc = max(1, min(4, 16 // DC))
    NSLAB = (HF // 128) // GKc
    d1bf = ap(nc.dram_tensor("d1bf", [HF // 512, 128, KT * 512], BF16, kind="Internal"))
    d2bf = ap(nc.dram_tensor("d2bf", [NSLAB, 128, GKc * DC * 512], BF16, kind="Internal"))
    qrope = ap(nc.dram_tensor("qrope", [T, D], BF16, kind="Internal"))
    sgd = ap(nc.dram_tensor("sgd", [S, T, D], BF16, kind="Internal"))
    d1v = w["d1w"].rearrange("(a p) n -> p a n", p=128)
    d2v = w["d2w"].rearrange("(a p) n -> p a n", p=128)
    aggv = w["aggw"].rearrange("(a p) n -> p a n", p=128)
    bcv = w["bcw"].rearrange("(a p) n -> p a n", p=128)
    gwv = w["gw"].rearrange("(a p) n -> p a n", p=128)
    qkv_ = w["qk"].rearrange("(a p) n -> p a n", p=128)
    kkv = w["kk"].rearrange("(a p) n -> p a n", p=128)
    vkv = w["vk"].rearrange("(a p) n -> p a n", p=128)
    okv = w["ok"].rearrange("(a p) n -> p a n", p=128)

    with ExitStack() as st:
        b = B(nc, st)
        sb = lambda n, s, d=F32: st.enter_context(nc.sbuf_tensor(n, s, d))
        psb = lambda n, s, d=F32: st.enter_context(nc.psum_tensor(n, s, d))

        pD0 = psb("pD0", [128, 1024])
        pD1 = psb("pD1", [128, 1024])
        pB = psb("pB", [128, 2048], BF16)
        pC0 = psb("pC0", [128, 512])
        pC1 = psb("pC1", [128, 512])

        def PD(c):
            t = (pD0, ("pD0",)) if (c // 2) % 2 == 0 else (pD1, ("pD1",))
            return t[0], (c % 2) * 512, t[1]

        ident = sb("ident", [128, 128], BF16)
        ones1 = sb("ones1", [128, 128], BF16)
        onec = sb("onec", [128, 1])
        b.op("gpsimd", lambda e: e.memset(ident[:, :], 0.0), writes=[("ident",)])
        b.op("gpsimd", lambda e: e.affine_select(
            out=ident[:, :], in_=ident[:, :], compare_op=ALU.not_equal, fill=1.0,
            base=0, pattern=[[-1, 128]], channel_multiplier=1),
            reads=[("ident",)], writes=[("ident",)])
        b.op("gpsimd", lambda e: e.memset(ones1[:, :], 1.0), writes=[("ones1",)])
        b.op("gpsimd", lambda e: e.memset(onec[:, :], 1.0), writes=[("onec",)])

        xt = sb("xt", [128, 3, D])
        st6 = sb("st6", [128, 4, NHX, 6])
        sth2 = sb("sth2", [128, 4, NHX, 2])
        st2 = sb("st2", [128, 4, 2])
        mu = sb("mu", [128, 4])
        rr = sb("rr", [128, 4])
        nmr = sb("nmr", [128, 4])
        n1 = sb("n1", [128, 4])
        n2 = sb("n2", [128, 4])
        msh = sb("msh", [128, 4, NHX])
        xnbf = sb("xnbf", [128, 2, max(D, CB)], BF16)
        xfm = sb("xfm", [128, 2, KT, 128], BF16)
        u0 = sb("u0", [128, max(D, CB)])
        u1 = sb("u1", [128, max(D, CB)])
        wbig = sb("wbig", [128, max(KT, NH), D], BF16)

        def rsqrt_col(v_ap, v_key, sl, eps, rows=128):
            b.op("vector", lambda e: e.tensor_scalar(n1[0:rows, sl:sl + 1], v_ap, 1.0,
                                                     eps, ALU.mult, ALU.add),
                 reads=[v_key], writes=[("n1", sl)])
            b.op("scalar", lambda e: e.activation(n2[0:rows, sl:sl + 1],
                                                  n1[0:rows, sl:sl + 1], AF.Ln),
                 reads=[("n1", sl)], writes=[("n2", sl)])
            b.op("scalar", lambda e: e.activation(rr[0:rows, sl:sl + 1],
                                                  n2[0:rows, sl:sl + 1], AF.Exp,
                                                  scale=-0.5),
                 reads=[("n2", sl)], writes=[("rr", sl)])

        def ln_stats(src_ap, src_key, sl, eps, rows=128, neg_mu_r=False):
            chunk = min(512, D)
            nch = D // chunk
            for c in range(nch):
                b.op("vector", lambda e, c=c: e.bn_stats(
                    st6[0:rows, sl, c, :], src_ap[:, c * chunk:(c + 1) * chunk]),
                    reads=[src_key], writes=[("st6", sl)])
            b.op("vector", lambda e, nch=nch: e.bn_aggr(st2[0:rows, sl, :],
                                                        st6[0:rows, sl, 0:nch, :]),
                 reads=[("st6", sl)], writes=[("st2", sl)])
            b.op("vector", lambda e: e.tensor_copy(mu[0:rows, sl:sl + 1],
                                                   st2[0:rows, sl, 0:1]),
                 reads=[("st2", sl)], writes=[("mu", sl)])
            rsqrt_col(st2[0:rows, sl, 1:2], ("st2", sl), sl, eps, rows)
            if neg_mu_r:
                b.op("vector", lambda e: e.tensor_scalar(
                    nmr[0:rows, sl:sl + 1], mu[0:rows, sl:sl + 1],
                    rr[0:rows, sl:sl + 1], -1.0, ALU.mult, ALU.mult),
                    reads=[("mu", sl), ("rr", sl)], writes=[("nmr", sl)])

        def transpose_tile(src_ap, src_key, tslot, dst_ap=None, dst_key=None):
            for k in range(KT):
                b.op("tensor", lambda e, k=k: e.transpose(
                    pB[:, k * 128:(k + 1) * 128], src_ap[:, k * 128:(k + 1) * 128],
                    ident[:, :]),
                    reads=[src_key, ("ident",)], writes=[("pB",)])
            if dst_ap is None:
                dst_ap = xfm[:, tslot, :, :].rearrange("p a b -> p (a b)")[:, 0:D]
                dst_key = ("xfm", tslot)
            b.op("vector", lambda e, dst_ap=dst_ap, dst_key=dst_key: e.tensor_copy(
                dst_ap, pB[:, 0:D]),
                reads=[("pB",)], writes=[dst_key])

        def ffn(src, src_name, dst, dst_name, prep):
            tag = "_" + dst_name
            with ExitStack() as sc:
                sbl = lambda n, s, d=F32: sc.enter_context(nc.sbuf_tensor(n + tag, s, d))
                win = sbl("win", [128, 3, KT, 512], BF16)
                wbias = sbl("wbias", [1, 3, 512], BF16)
                usig = sbl("usig", [128, 2, D], BF16)
                sA = sbl("sA", [128, S * TT, 2])
                tfm = sbl("tfm", [128, KT, 2], BF16)
                t2fm = sbl("t2fm", [128, HF // 128, 2], BF16)
                png2 = sbl("png2", [2, D], BF16)
                pnb2 = sbl("pnb2", [2, D], BF16)
                brows = sbl("brows", [128, D], BF16)
                gOB = sbl("gOB", [128, D], BF16)
                bOB = sbl("bOB", [128, D], BF16)
                bcB = sbl("bcB", [128, max(D, CB)], BF16)
                vrow = win[0:1, 0, 0:(D // 512), :].rearrange("p a b -> p (a b)")
                th = u0
                ua = u1
                tnb = bcB[0:2, :]


                def bcast_row(dst_ap, dst_key, width, scale=None):
                    for c in range(width // 512):
                        p, off, pk = PD(c)
                        b.op("tensor", lambda e, c=c, p=p, off=off: e.matmul(
                            p[:, off:off + 512], ones1[0:1, :],
                            vrow[:, c * 512:(c + 1) * 512], start=True, stop=True),
                            reads=[("win", 0), ("ones1",)], writes=[pk])
                    for c in range(width // 512):
                        p, off, pk = PD(c)
                        if scale is None:
                            b.op("vector", lambda e, c=c, p=p, off=off: e.tensor_copy(
                                dst_ap[:, c * 512:(c + 1) * 512], p[:, off:off + 512]),
                                reads=[pk], writes=[dst_key])
                        else:
                            b.op("vector", lambda e, c=c, p=p, off=off: e.tensor_scalar(
                                dst_ap[:, c * 512:(c + 1) * 512], p[:, off:off + 512],
                                scale, None, ALU.mult),
                                reads=[pk], writes=[dst_key])

                def load_2row(dstt, dst_key, srcv, n):
                    b.dma("gpsimd", dstt[0:1, 0:n], srcv[None, 0:n], writes=[dst_key])
                    b.dma("gpsimd", dstt[1:2, 0:n], srcv[None, 0:n], writes=[dst_key])

                load_2row(png2, ("png2",), w["ing"], D)
                load_2row(pnb2, ("pnb2",), w["inb"], D)
                b.dma("gpsimd", brows[0:1, :], w["d2b"][None, :], writes=[("brows",)])
                b.dma("gpsimd", brows[32:33, :], w["aggb"][None, :], writes=[("brows",)])
                b.dma("gpsimd", brows[64:65, :], w["bcb"][None, :], writes=[("brows",)])
                b.dma("gpsimd", brows[96:97, :], w["gb"][None, :], writes=[("brows",)])

                def bias_mm(p, off, pk, row, cslice, m):
                    tp = (row, 0) if row else None
                    b.op("tensor", lambda e: e.matmul(
                        p[0:m, off:off + 512], ones1[row:row + 1, 0:m],
                        brows[row:row + 1, cslice:cslice + 512],
                        start=False, stop=True, tile_position=tp),
                        reads=[("brows",), ("ones1",)], writes=[pk])
                b.dma("gpsimd", wbig[:, 0:KT, :], gwv[:, :, :], writes=[("wbig",)])

                def tr2(src_ap, src_key, dst_ap, dst_key, ncols):
                    nk = ncols // 128
                    for k in range(nk):
                        b.op("tensor", lambda e, k=k: e.transpose(
                            pB[:, 2 * k:2 * k + 2],
                            src_ap[0:2, k * 128:(k + 1) * 128], ident[0:2, 0:2]),
                            reads=[src_key, ("ident",)], writes=[("pB",)])
                    b.op("vector", lambda e: e.tensor_copy(
                        dst_ap[:, 0:2 * nk], pB[:, 0:2 * nk]),
                        reads=[("pB",)], writes=[dst_key])

                def ln2row(state_ap, state_key, g2, g2k, b2, b2k, eps):
                    ln_stats(state_ap, state_key, 3, eps, rows=2)
                    b.op("vector", lambda e: e.tensor_scalar(
                        ua[0:2, 0:D], state_ap, mu[0:2, 3:4], rr[0:2, 3:4],
                        ALU.subtract, ALU.mult),
                        reads=[state_key, ("mu", 3), ("rr", 3)], writes=[("u1",)])
                    b.op("vector", lambda e: e.tensor_mul(ua[0:2, 0:D], ua[0:2, 0:D], g2),
                         reads=[("u1",), g2k], writes=[("u1",)])
                    b.op("vector", lambda e: e.tensor_tensor(tnb[0:2, 0:D], ua[0:2, 0:D],
                                                             b2, ALU.add),
                         reads=[("u1",), b2k], writes=[("bcB",)])

                def mm_cols(wview, brow_row, kt_total, n_out):
                    for cc in range(n_out // 512):
                        slot = cc % 3
                        b.dma("gpsimd", win[:, slot, 0:KT, :],
                              wview[:, :, cc * 512:(cc + 1) * 512],
                              writes=[("win", slot)])
                        p, off, pk = PD(cc)
                        for kt in range(kt_total):
                            b.op("tensor", lambda e, kt=kt, slot=slot, p=p, off=off: e.matmul(
                                p[0:2, off:off + 512], tfm[:, kt, :],
                                win[:, slot, kt, :],
                                start=(kt == 0), stop=False),
                                reads=[("tfm",), ("win", slot)], writes=[pk])
                        bias_mm(p, off, pk, brow_row, cc * 512, 2)
                        pump(1)

                def stageA1(s, it):
                    n = s * TT + it
                    slot = n % 3
                    xsl = slot
                    tkey = ("xt", xsl)
                    b.dma("sync", xt[:, xsl, :], src[s, it * 128:(it + 1) * 128, :],
                          reads=[(src_name, s, it)], writes=[tkey])
                    ln_stats(xt[:, xsl, :], tkey, slot, 1e-5, neg_mu_r=True)
                    b.op("vector", lambda e, n=n, slot=slot: e.tensor_copy(
                        sA[:, n, 0:1], nmr[:, slot:slot + 1]),
                        reads=[("nmr", slot)], writes=[("sA", n)])
                    b.op("vector", lambda e, n=n, slot=slot: e.tensor_copy(
                        sA[:, n, 1:2], rr[:, slot:slot + 1]),
                        reads=[("rr", slot)], writes=[("sA", n)])

                def stageA2(s, it):
                    n = s * TT + it
                    slot = n % 3
                    xsl = slot
                    fsl = n % 2
                    b.op("scalar", lambda e, slot=slot, xsl=xsl, fsl=fsl: e.activation(
                        usig[:, fsl, 0:D], xt[:, xsl, :], AF.Identity,
                        bias=nmr[:, slot:slot + 1], scale=rr[:, slot:slot + 1]),
                        reads=[("xt", xsl), ("nmr", slot), ("rr", slot)],
                        writes=[("usig", fsl)])
                    transpose_tile(usig[:, fsl, 0:D], ("usig", fsl), 0,
                                   dst_ap=xfm[:, fsl].rearrange("p a b -> p (a b)")[:, 0:D],
                                   dst_key=("xfm", fsl))

                PCs = [(pC0, ("pC0",)), (pC1, ("pC1",))]

                def stageB_chunk(s, it, c, fsl):
                    p, pk = PCs[c % 2]
                    for kt in range(KT):
                        b.op("tensor", lambda e, c=c, kt=kt, p=p, fsl=fsl: e.matmul(
                            p[:, 0:512], xfm[:, fsl, kt, :],
                            wbig[:, kt, c * 512:(c + 1) * 512],
                            start=(kt == 0), stop=False),
                            reads=[("xfm", fsl), ("wbig",)], writes=[pk])
                    bias_mm(p, 0, pk, 96, c * 512, 128)
                    b.op("scalar", lambda e, c=c, p=p, fsl=fsl: e.activation(
                        usig[:, fsl, c * 512:(c + 1) * 512], p[:, 0:512],
                        AF.Exp, scale=-1.0),
                        reads=[pk], writes=[("usig", fsl)])

                def sig_tail(s, it, fsl):
                    b.op("scalar", lambda e, fsl=fsl: e.activation(
                        usig[:, fsl, 0:D], usig[:, fsl, 0:D],
                        AF.Ln, bias=onec[:, 0:1]),
                        reads=[("usig", fsl), ("onec",)], writes=[("usig", fsl)])
                    b.op("scalar", lambda e, fsl=fsl: e.activation(
                        usig[:, fsl, 0:D], usig[:, fsl, 0:D],
                        AF.Exp, scale=-1.0),
                        reads=[("usig", fsl)], writes=[("usig", fsl)])
                    b.dma("gpsimd", sgd[s, it * 128:(it + 1) * 128, :],
                          usig[:, fsl, 0:D],
                          reads=[("usig", fsl)], writes=[("sgd", s, it)])

                def b1_gen():
                    tiles = [(s_, it_) for s_ in range(S) for it_ in range(TT)]
                    stageA1(*tiles[0])
                    yield
                    stageA2(*tiles[0])
                    yield
                    if len(tiles) > 1:
                        stageA1(*tiles[1])
                        yield
                    for n, (s_, it_) in enumerate(tiles):
                        for c in range(DC):
                            stageB_chunk(s_, it_, c, n % 2)
                            yield
                        if n + 2 < len(tiles):
                            stageA1(*tiles[n + 2])
                            yield
                        sig_tail(s_, it_, n % 2)
                        yield
                        if n + 1 < len(tiles):
                            stageA2(*tiles[n + 1])
                            yield

                gen = b1_gen()
                gen_done = [False]

                def pump(k=1):
                    for _ in range(k):
                        if gen_done[0]:
                            return
                        try:
                            next(gen)
                        except StopIteration:
                            gen_done[0] = True
                            return

                for s in range(S):
                    b.dma("sync", ua[s:s + 1, 0:D], src[s, T - 1:T, :],
                          reads=[(src_name, s, TT - 1)], writes=[("u1",)])
                ln_stats(ua[0:2, 0:D], ("u1",), 3, 1e-5, rows=2)
                b.op("vector", lambda e: e.tensor_scalar(ua[0:2, 0:D], ua[0:2, 0:D],
                                                         mu[0:2, 3:4], rr[0:2, 3:4],
                                                         ALU.subtract, ALU.mult),
                     reads=[("u1",), ("mu", 3), ("rr", 3)], writes=[("u1",)])
                ln_stats(ua[0:2, 0:D], ("u1",), 3, 1e-6, rows=2)
                b.op("vector", lambda e: e.tensor_scalar(ua[0:2, 0:D], ua[0:2, 0:D],
                                                         mu[0:2, 3:4], rr[0:2, 3:4],
                                                         ALU.subtract, ALU.mult),
                     reads=[("u1",), ("mu", 3), ("rr", 3)], writes=[("u1",)])
                b.op("vector", lambda e: e.tensor_mul(ua[0:2, 0:D], ua[0:2, 0:D],
                                                      png2[0:2, 0:D]),
                     reads=[("u1",), ("png2",)], writes=[("u1",)])
                b.op("vector", lambda e: e.tensor_tensor(tnb[0:2, 0:D], ua[0:2, 0:D],
                                                         pnb2[0:2, 0:D], ALU.add),
                     reads=[("u1",), ("pnb2",)], writes=[("bcB",)])
                load_2row(png2, ("png2",), w["png"], D)
                load_2row(pnb2, ("pnb2",), w["pnb"], D)
                tr2(tnb, ("bcB",), tfm.rearrange("p a b -> p (a b)"), ("tfm",), D)
                mm_cols(aggv, 32, KT, D)
                for c in range(DC):
                    p, off, pk = PD(c)
                    b.op("vector", lambda e, c=c, p=p, off=off: e.tensor_copy(
                        th[0:2, c * 512:(c + 1) * 512], p[0:2, off:off + 512]),
                        reads=[pk], writes=[("u0",)])

                for ks in range(KSTEPS):
                    ln2row(th[0:2, 0:D], ("u0",), png2[0:2, 0:D], ("png2",),
                           pnb2[0:2, 0:D], ("pnb2",), 1e-6)
                    tr2(tnb, ("bcB",), tfm.rearrange("p a b -> p (a b)"), ("tfm",), D)
                    slot = 0
                    for cb0 in range(0, HF, CB):
                        for cc in range(CB // 512):
                            c0 = cb0 + cc * 512
                            slot = (cb0 // 512 + cc) % 3
                            b.dma("gpsimd", wbias[0:1, slot, 0:512],
                                  w["d1b"][None, c0:c0 + 512], writes=[("wbias", slot)])
                            if prep and ks == 0:
                                b.dma("gpsimd", win[:, slot, 0:KT, :],
                                      d1v[:, :, c0:c0 + 512], writes=[("win", slot)])
                            else:
                                b.dma("sync" if cc % 2 == 0 else "scalar",
                                      win[:, slot, 0:KT, :],
                                      d1bf[c0 // 512].rearrange("p (a n) -> p a n", n=512),
                                      reads=[("d1bf", c0)], writes=[("win", slot)])
                            p, off, pk = PD(cc)
                            for kt in range(KT):
                                b.op("tensor", lambda e, kt=kt, slot=slot, p=p, off=off: e.matmul(
                                    p[0:2, off:off + 512], tfm[:, kt, :],
                                    win[:, slot, kt, :],
                                    start=(kt == 0), stop=False),
                                    reads=[("tfm",), ("win", slot)], writes=[pk])
                            b.op("tensor", lambda e, p=p, off=off, slot=slot: e.matmul(
                                p[0:2, off:off + 512], ones1[0:1, 0:2],
                                wbias[0:1, slot, 0:512],
                                start=False, stop=True),
                                reads=[("wbias", slot)], writes=[pk])
                            if prep and ks == 0:
                                b.dma("scalar",
                                      d1bf[c0 // 512].rearrange("p (a n) -> p a n", n=512),
                                      win[:, slot, 0:KT, :],
                                      reads=[("win", slot)], writes=[("d1bf", c0)])
                            pump(1)
                        for half in range(CB // 1024):
                            p = pD0 if half == 0 else pD1
                            pk = ("pD0",) if half == 0 else ("pD1",)
                            b.op("scalar", lambda e, half=half, p=p: e.activation(
                                tnb[0:2, half * 1024:(half + 1) * 1024],
                                p[0:2, 0:1024], AF.Gelu),
                                reads=[pk], writes=[("bcB",)])
                        tr2(tnb[0:2, 0:CB], ("bcB",),
                            t2fm.rearrange("p a b -> p (a b)")[:, 2 * cb0 // 128:],
                            ("t2fm",), CB)
                        pump(1)
                    nk2 = HF // 128
                    GK = max(1, min(4, 16 // DC))
                    for kt0 in range(0, nk2, GK):
                        slot2 = (kt0 // GK) % 3
                        gsz = min(GK, nk2 - kt0)
                        slab = win[:, slot2, 0:gsz * DC, :].rearrange(
                            "p (a c) n -> p a c n", c=DC)
                        dview_w = w["d2w"][kt0 * 128:(kt0 + gsz) * 128, :].rearrange(
                            "(a p) (c n) -> p a c n", p=128, n=512)
                        dview_b = d2bf[kt0 // GK].rearrange(
                            "p (a c n) -> p a c n", c=DC, n=512)
                        if prep and ks == 0:
                            b.dma("gpsimd", slab, dview_w, writes=[("win", slot2)])
                            b.dma("scalar", dview_b, slab,
                                  reads=[("win", slot2)], writes=[("d2bf", kt0)])
                        else:
                            b.dma("sync" if (kt0 // GK) % 2 == 0 else "scalar", slab,
                                  dview_b,
                                  reads=[("d2bf", kt0)], writes=[("win", slot2)])
                        pump(1)
                        for g in range(gsz):
                            kt = kt0 + g
                            for cc in range(DC):
                                p, off, pk = PD(cc)
                                b.op("tensor", lambda e, kt=kt, g=g, cc=cc, slot2=slot2, p=p, off=off: e.matmul(
                                    p[0:2, off:off + 512], t2fm[:, kt, :],
                                    win[:, slot2, g * DC + cc, :],
                                    start=(kt == 0), stop=False),
                                    reads=[("t2fm",), ("win", slot2)], writes=[pk])
                        pump(1)
                    for cc in range(DC):
                        p, off, pk = PD(cc)
                        bias_mm(p, off, pk, 0, cc * 512, 2)
                    for cc in range(DC):
                        p, off, pk = PD(cc)
                        b.op("vector", lambda e, cc=cc, p=p, off=off: e.tensor_tensor(
                            th[0:2, cc * 512:(cc + 1) * 512],
                            th[0:2, cc * 512:(cc + 1) * 512],
                            p[0:2, off:off + 512], ALU.add),
                            reads=[("u0",), pk], writes=[("u0",)])

                b.op("vector", lambda e: e.tensor_copy(tnb[0:2, 0:D], th[0:2, 0:D]),
                     reads=[("u0",)], writes=[("bcB",)])
                tr2(tnb, ("bcB",), tfm.rearrange("p a b -> p (a b)"), ("tfm",), D)
                mm_cols(bcv, 64, KT, D)
                for c in range(DC):
                    p, off, pk = PD(c)
                    b.op("vector", lambda e, c=c, p=p, off=off: e.tensor_copy(
                        ua[0:2, c * 512:(c + 1) * 512], p[0:2, off:off + 512]),
                        reads=[pk], writes=[("u1",)])
                b.op("vector", lambda e: e.tensor_copy(brows[32:34, 0:D], ua[0:2, 0:D]),
                     reads=[("u1",)], writes=[("brows",)])

                while not gen_done[0]:
                    pump(1)
                b.dma("gpsimd", vrow[:, 0:D], w["outg"][None, :], writes=[("win", 0)])
                bcast_row(gOB, ("gOB",), D, scale=0.5)
                b.dma("gpsimd", vrow[:, 0:D], w["outb"][None, :], writes=[("win", 0)])
                bcast_row(bOB, ("bOB",), D, scale=0.5)

                def b2A(s, it):
                    xsl = it % 3
                    tkey = ("xt", xsl)
                    b.dma("sync", xt[:, xsl, :], src[s, it * 128:(it + 1) * 128, :],
                          reads=[(src_name, s, it)], writes=[tkey])
                    b.dma("scalar", xnbf[:, it % 2, 0:D], sgd[s, it * 128:(it + 1) * 128, :],
                          reads=[("sgd", s, it)], writes=[("xnbf", it % 2)])

                def b2C(s, it):
                    n = s * TT + it
                    slot = it % 3
                    xsl = it % 3
                    ut, uk = (u0, ("u0",)) if n % 2 == 0 else (u1, ("u1",))
                    b.op("scalar", lambda e, n=n, xsl=xsl, ut=ut: e.activation(
                        ut[:, 0:D], xt[:, xsl, :], AF.Identity,
                        bias=sA[:, n, 0:1], scale=sA[:, n, 1:2]),
                        reads=[("xt", xsl), ("sA", n)],
                        writes=[uk])
                    b.op("vector", lambda e, it=it: e.tensor_mul(
                        xnbf[:, it % 2, 0:D], xnbf[:, it % 2, 0:D], bcB[:, 0:D]),
                        reads=[("xnbf", it % 2), ("bcB",)], writes=[("xnbf", it % 2)])
                    b.op("vector", lambda e, it=it, ut=ut: e.tensor_tensor(
                        ut[:, 0:D], ut[:, 0:D], xnbf[:, it % 2, 0:D], ALU.add),
                        reads=[uk, ("xnbf", it % 2)], writes=[uk])
                    ln_stats(ut[:, 0:D], uk, slot, 1e-6, neg_mu_r=True)
                    fsl = n % 2
                    b.op("scalar", lambda e, slot=slot, ut=ut, fsl=fsl: e.activation(
                        usig[:, fsl, 0:D], ut[:, 0:D], AF.Identity,
                        bias=nmr[:, slot:slot + 1], scale=rr[:, slot:slot + 1]),
                        reads=[uk, ("nmr", slot), ("rr", slot)], writes=[("usig", fsl)])
                    b.op("vector", lambda e, fsl=fsl: e.tensor_mul(
                        usig[:, fsl, 0:D], usig[:, fsl, 0:D], gOB[:, 0:D]),
                        reads=[("usig", fsl), ("gOB",)], writes=[("usig", fsl)])
                    b.op("vector", lambda e, fsl=fsl: e.tensor_tensor(
                        usig[:, fsl, 0:D], usig[:, fsl, 0:D], bOB[:, 0:D], ALU.add),
                        reads=[("usig", fsl), ("bOB",)], writes=[("usig", fsl)])
                    b.op("vector", lambda e, ut=ut, fsl=fsl, xsl=xsl: e.tensor_tensor(
                        ut[:, 0:D], usig[:, fsl, 0:D], xt[:, xsl, :], ALU.add),
                        reads=[("usig", fsl), ("xt", xsl), uk], writes=[uk])
                    b.dma("scalar", dst[s, it * 128:(it + 1) * 128, :], ut[:, 0:D],
                          reads=[uk], writes=[(dst_name, s, it)])

                for s_ in range(S):
                    b.dma("gpsimd", vrow[:, 0:D], brows[32 + s_:33 + s_, 0:D],
                          reads=[("brows",)], writes=[("win", 0)])
                    bcast_row(bcB, ("bcB",), D)
                    b2A(s_, 0)
                    b2A(s_, 1)
                    for it_ in range(TT):
                        b2C(s_, it_)
                        if it_ + 2 < TT:
                            b2A(s_, it_ + 2)
            b.barrier()

        def attn(src, src_name, dst, dst_name):
            with ExitStack() as sc:
                sbl = lambda n, s, d=F32: sc.enter_context(nc.sbuf_tensor(n + "_at", s, d))
                cosTM = sbl("cosTM", [128, T], BF16)
                sinTM = sbl("sinTM", [128, T], BF16)
                with ExitStack() as st_init:
                    sbi = lambda n, s, d=F32: st_init.enter_context(nc.sbuf_tensor(n, s, d))
                    trow = sbi("trow", [1, T])
                    irow = sbi("irow", [1, HD])
                    rtmp = sbi("rtmp", [1, HD])
                    wr0 = sbi("wr0", [128, 128])
                    wr1 = sbi("wr1", [128, 128])
                    wr2 = sbi("wr2", [128, 128])
                    wr3 = sbi("wr3", [128, 128])
                    b.op("gpsimd", lambda e: e.iota(trow[:, :], pattern=[[1, T]], base=0,
                                                    channel_multiplier=0,
                                                    allow_small_or_imprecise_dtypes=True),
                         writes=[("trow",)])
                    b.op("gpsimd", lambda e: e.iota(irow[:, :], pattern=[[1, HD]], base=0,
                                                    channel_multiplier=0,
                                                    allow_small_or_imprecise_dtypes=True),
                         writes=[("irow",)])
                    b.op("vector", lambda e: e.tensor_single_scalar(rtmp[:, :], irow[:, :],
                                                                    64.0, ALU.is_ge),
                         reads=[("irow",)], writes=[("rtmp",)])
                    b.op("vector", lambda e: e.scalar_tensor_tensor(irow[:, :], rtmp[:, :], -64.0,
                                                                    irow[:, :], ALU.mult, ALU.add),
                         reads=[("rtmp",), ("irow",)], writes=[("irow",)])
                    b.op("vector", lambda e: e.tensor_scalar(rtmp[:, :], irow[:, :],
                                                             -LN1E4_LO / 64.0, None, ALU.mult),
                         reads=[("irow",)], writes=[("rtmp",)])
                    b.op("vector", lambda e: e.scalar_tensor_tensor(irow[:, :], irow[:, :],
                                                                    -LN1E4_HI / 64.0, rtmp[:, :],
                                                                    ALU.mult, ALU.add),
                         reads=[("irow",), ("rtmp",)], writes=[("irow",)])
                    b.op("scalar", lambda e: e.activation(irow[:, :], irow[:, :], AF.Exp),
                         reads=[("irow",)], writes=[("irow",)])
                    nstages = max(1, int(math.ceil(math.log2(max(2.0, T / TWO_PI)))))
                    for i in range(TT):
                        b.op("tensor", lambda e, i=i: e.matmul(
                            pD0[:, 0:128], trow[:, i * 128:(i + 1) * 128], irow[:, :],
                            start=True, stop=True),
                            reads=[("trow",), ("irow",)], writes=[("pD0",)])
                        b.op("vector", lambda e: e.tensor_copy(wr1[:, :], pD0[:, 0:128]),
                             reads=[("pD0",)], writes=[("wr1",)])
                        b.op("vector", lambda e: e.memset(wr2[:, :], 0.0), writes=[("wr2",)])
                        for j in range(nstages - 1, -1, -1):
                            cj = TWO_PI_HI * float(2 ** j)
                            b.op("vector", lambda e, cj=cj: e.tensor_single_scalar(
                                wr3[:, :], wr1[:, :], cj, ALU.is_ge),
                                reads=[("wr1",)], writes=[("wr3",)])
                            b.op("vector", lambda e, cj=cj: e.scalar_tensor_tensor(
                                wr1[:, :], wr3[:, :], -cj, wr1[:, :], ALU.mult, ALU.add),
                                reads=[("wr3",), ("wr1",)], writes=[("wr1",)])
                            b.op("vector", lambda e, j=j: e.scalar_tensor_tensor(
                                wr2[:, :], wr3[:, :], float(2 ** j), wr2[:, :],
                                ALU.mult, ALU.add),
                                reads=[("wr3",), ("wr2",)], writes=[("wr2",)])
                        b.op("vector", lambda e: e.scalar_tensor_tensor(
                            wr1[:, :], wr2[:, :], -TWO_PI_LO, wr1[:, :], ALU.mult, ALU.add),
                            reads=[("wr2",), ("wr1",)], writes=[("wr1",)])
                        b.op("vector", lambda e: e.tensor_single_scalar(
                            wr3[:, :], wr1[:, :], PI_F, ALU.is_ge),
                            reads=[("wr1",)], writes=[("wr3",)])
                        b.op("vector", lambda e: e.scalar_tensor_tensor(
                            wr0[:, :], wr3[:, :], -TWO_PI_HI, wr1[:, :], ALU.mult, ALU.add),
                            reads=[("wr3",), ("wr1",)], writes=[("wr0",)])
                        b.op("scalar", lambda e, i=i: e.activation(
                            sinTM[:, i * 128:(i + 1) * 128], wr0[:, :], AF.Sin),
                            reads=[("wr0",)], writes=[("sinTM",)])
                        b.op("vector", lambda e, i=i: e.tensor_scalar(
                            sinTM[:, i * 128:i * 128 + 64], sinTM[:, i * 128:i * 128 + 64],
                            -1.0, None, ALU.mult),
                            reads=[("sinTM",)], writes=[("sinTM",)])
                        b.op("vector", lambda e: e.tensor_scalar(
                            wr0[:, :], wr1[:, :], 1.0, PI_F / 2.0, ALU.mult, ALU.add),
                            reads=[("wr1",)], writes=[("wr0",)])
                        b.op("vector", lambda e: e.tensor_single_scalar(
                            wr3[:, :], wr0[:, :], PI_F, ALU.is_ge),
                            reads=[("wr0",)], writes=[("wr3",)])
                        b.op("vector", lambda e: e.scalar_tensor_tensor(
                            wr0[:, :], wr3[:, :], -TWO_PI_HI, wr0[:, :], ALU.mult, ALU.add),
                            reads=[("wr3",), ("wr0",)], writes=[("wr0",)])
                        b.op("scalar", lambda e, i=i: e.activation(
                            cosTM[:, i * 128:(i + 1) * 128], wr0[:, :], AF.Sin),
                            reads=[("wr0",)], writes=[("cosTM",)])
                    b.barrier()

                kvt = sbl("kvt", [128, KT, KVW], BF16)
                qbuf = sbl("qbuf", [128, 2, D], BF16)
                k_fm = sbl("k_fm", [128, NKV, T], BF16)
                v_tm = sbl("v_tm", [128, TT, KVW], BF16)
                cq = sbl("cq", [128, TT, NH], BF16)
                sums = sbl("sums", [128, NH])
                sacc = sbl("sacc", [128, 2, 2])
                rsum = sbl("rsum", [128, NH])
                probs = sbl("probs", [128, 2, max(T, D)], BF16)
                qsb = probs[:, 0, 0:D]
                ksb = probs[:, 1, 0:KVW]
                probsT = sbl("probsT", [128, 2, TT, 128], BF16)
                encT = sbl("encT", [128, NH, 128], BF16)

                def rms_rope(src_sb, src_sb_key, width, ctile, stile,
                             out_ap, out_key, extra_scale, want_cq, it, sl):
                    nch = width // 128
                    for c in range(nch):
                        b.op("vector", lambda e, c=c: e.bn_stats(
                            st6[:, sl, c, :], src_sb[:, c * 128:(c + 1) * 128]),
                            reads=[src_sb_key], writes=[("st6", sl)])
                    for h in range(nch):
                        b.op("vector", lambda e, h=h: e.bn_aggr(
                            sth2[:, sl, h, :], st6[:, sl, h:h + 1, :]),
                            reads=[("st6", sl)], writes=[("sth2", sl)])
                    b.op("vector", lambda e, nch=nch: e.tensor_mul(
                        msh[:, sl, 0:nch], sth2[:, sl, 0:nch, 0], sth2[:, sl, 0:nch, 0]),
                        reads=[("sth2", sl)], writes=[("msh", sl)])
                    b.op("vector", lambda e, nch=nch: e.tensor_tensor(
                        msh[:, sl, 0:nch], msh[:, sl, 0:nch], sth2[:, sl, 0:nch, 1],
                        ALU.add),
                        reads=[("msh", sl), ("sth2", sl)], writes=[("msh", sl)])
                    b.op("vector", lambda e, nch=nch: e.tensor_reduce(
                        st2[:, sl, 0:1], msh[:, sl, 0:nch], AX, ALU.add),
                        reads=[("msh", sl)], writes=[("st2", sl)])
                    b.op("vector", lambda e, nch=nch: e.tensor_scalar(
                        st2[:, sl, 0:1], st2[:, sl, 0:1], 1.0 / nch, None, ALU.mult),
                        reads=[("st2", sl)], writes=[("st2", sl)])
                    rsqrt_col(st2[:, sl, 0:1], ("st2", sl), sl, 1e-6)
                    srcv = src_sb.rearrange("p (a n) -> p a n", n=128)
                    u0v = u0[:, 0:width].rearrange("p (a n) -> p a n", n=128)
                    u1v = u1[:, 0:width].rearrange("p (a n) -> p a n", n=128)
                    cb_ = ctile.unsqueeze(1).broadcast_to([128, nch, 128])
                    sb0 = stile[:, 0:64].unsqueeze(1).broadcast_to([128, nch, 64])
                    sb1 = stile[:, 64:128].unsqueeze(1).broadcast_to([128, nch, 64])
                    b.op("vector", lambda e: e.tensor_tensor(u0v, srcv, cb_, ALU.mult),
                         reads=[src_sb_key, ("cosTM",)], writes=[("u0",)])
                    b.op("vector", lambda e: e.tensor_tensor(
                        u1v[:, :, 0:64], srcv[:, :, 64:128], sb0, ALU.mult),
                        reads=[src_sb_key, ("sinTM",)], writes=[("u1",)])
                    b.op("vector", lambda e: e.tensor_tensor(
                        u1v[:, :, 64:128], srcv[:, :, 0:64], sb1, ALU.mult),
                        reads=[src_sb_key, ("sinTM",)], writes=[("u1",)])
                    b.op("vector", lambda e: e.tensor_tensor(
                        u0[:, 0:width], u0[:, 0:width], u1[:, 0:width], ALU.add),
                        reads=[("u0",), ("u1",)], writes=[("u0",)])
                    b.op("vector", lambda e: e.tensor_scalar(
                        out_ap, u0[:, 0:width], rr[:, sl:sl + 1], extra_scale,
                        ALU.mult, ALU.mult),
                        reads=[("u0",), ("rr", sl)], writes=[out_key])
                    if want_cq:
                        b.op("scalar", lambda e, nch=nch: e.activation(
                            msh[:, sl, 0:nch], msh[:, sl, 0:nch], AF.Ln),
                            reads=[("msh", sl)], writes=[("msh", sl)])
                        b.op("scalar", lambda e, nch=nch: e.activation(
                            msh[:, sl, 0:nch], msh[:, sl, 0:nch], AF.Exp, scale=0.5),
                            reads=[("msh", sl)], writes=[("msh", sl)])
                        b.op("vector", lambda e, nch=nch: e.tensor_scalar(
                            msh[:, sl, 0:nch], msh[:, sl, 0:nch], rr[:, sl:sl + 1],
                            NEG_CQC, ALU.mult, ALU.mult),
                            reads=[("msh", sl), ("rr", sl)], writes=[("msh", sl)])
                        b.op("vector", lambda e, it=it, nch=nch: e.tensor_copy(
                            cq[:, it, 0:nch], msh[:, sl, 0:nch]),
                            reads=[("msh", sl)], writes=[("cq", it)])

                def a_stage1(s, it):
                    slot = it % 2
                    xsl = it % 3
                    tkey = ("xt", xsl)
                    b.dma("sync", xt[:, xsl, :], src[s, it * 128:(it + 1) * 128, :],
                          reads=[(src_name, s, it)], writes=[tkey])
                    ln_stats(xt[:, xsl, :], tkey, slot, 1e-5, neg_mu_r=True)
                    b.op("scalar", lambda e, slot=slot, xsl=xsl: e.activation(
                        xnbf[:, slot, 0:D], xt[:, xsl, :], AF.Identity,
                        bias=nmr[:, slot:slot + 1], scale=rr[:, slot:slot + 1]),
                        reads=[tkey, ("nmr", slot), ("rr", slot)],
                        writes=[("xnbf", slot)])
                    transpose_tile(xnbf[:, slot, 0:D], ("xnbf", slot), slot)

                def a_proj(s, it):
                    slot = it % 2
                    for c in range(DC):
                        p, off, pk = PD(c)
                        for kt in range(KT):
                            b.op("tensor", lambda e, c=c, kt=kt, p=p, off=off, slot=slot: e.matmul(
                                p[:, off:off + 512], xfm[:, slot, kt, :],
                                wbig[:, kt, c * 512:(c + 1) * 512],
                                start=(kt == 0), stop=(kt == KT - 1)),
                                reads=[("xfm", slot), ("wbig",)], writes=[pk])
                    for kt in range(KT):
                        b.op("tensor", lambda e, kt=kt, slot=slot: e.matmul(
                            pC0[:, 0:KVW], xfm[:, slot, kt, :], kvt[:, kt, 0:KVW],
                            start=(kt == 0), stop=(kt == KT - 1)),
                            reads=[("xfm", slot), ("kvt",)], writes=[("pC0",)])

                def a_rope(s, it):
                    slot = it % 2
                    for c in range(DC):
                        p, off, pk = PD(c)
                        b.op("vector", lambda e, c=c, p=p, off=off: e.tensor_copy(
                            qsb[:, c * 512:(c + 1) * 512], p[:, off:off + 512]),
                            reads=[pk], writes=[("probs", 0)])
                    b.op("vector", lambda e: e.tensor_copy(ksb[:, 0:KVW], pC0[:, 0:KVW]),
                         reads=[("pC0",)], writes=[("probs", 1)])
                    ctile = cosTM[:, it * 128:(it + 1) * 128]
                    stile = sinTM[:, it * 128:(it + 1) * 128]
                    rms_rope(qsb, ("probs", 0), D, ctile, stile,
                             qbuf[:, slot, :], ("qbuf", slot), QSC, True, it, slot)
                    for k in range(KT):
                        b.op("tensor", lambda e, k=k, slot=slot: e.transpose(
                            pB[:, k * 128:(k + 1) * 128],
                            qbuf[:, slot, k * 128:(k + 1) * 128], ident[:, :]),
                            reads=[("qbuf", slot), ("ident",)], writes=[("pB",)])
                    b.op("vector", lambda e, slot=slot: e.tensor_copy(
                        qbuf[:, slot, 0:D], pB[:, 0:D]),
                        reads=[("pB",)], writes=[("qbuf", slot)])
                    b.dma("scalar", qrope[it * 128:(it + 1) * 128, :],
                          qbuf[:, slot, :], reads=[("qbuf", slot)],
                          writes=[("qrope", it)])
                    rms_rope(ksb, ("probs", 1), KVW, ctile, stile,
                             xnbf[:, slot, 0:KVW], ("xnbf", slot), 1.0, False, it, slot)
                    for h in range(NKV):
                        b.op("tensor", lambda e, h=h, slot=slot: e.transpose(
                            pB[:, h * 128:(h + 1) * 128],
                            xnbf[:, slot, h * 128:(h + 1) * 128], ident[:, :]),
                            reads=[("xnbf", slot), ("ident",)], writes=[("pB",)])
                    b.op("vector", lambda e, it=it: e.tensor_copy(
                        k_fm[:, 0:NKV, it * 128:(it + 1) * 128],
                        pB[:, 0:NKV * 128].rearrange("p (a b) -> p a b", b=128)),
                        reads=[("pB",)], writes=[("k_fm", it)])

                kf_reads = [("k_fm", i) for i in range(TT)]
                vt_reads = [("v_tm", i) for i in range(TT)]

                def c2_load(it):
                    slot = it % 2
                    b.dma("sync",
                          xfm[:, slot, :, :].rearrange("p a b -> p (a b)")[:, 0:D],
                          qrope[it * 128:(it + 1) * 128, :],
                          reads=[("qrope", it)], writes=[("xfm", slot)])

                def c2_stage1(it, h):
                    slot = it % 2
                    kv = h // G
                    ps_ = h % 2
                    for kb in range(NKB):
                        p = pD0 if (kb + NKB * h) % 2 == 0 else pD1
                        pk = ("pD0",) if (kb + NKB * h) % 2 == 0 else ("pD1",)
                        kch = min(512, KB)
                        for c in range(KB // kch):
                            b.op("tensor", lambda e, h=h, kb=kb, c=c, kv=kv, p=p, kch=kch, slot=slot: e.matmul(
                                p[:, c * kch:(c + 1) * kch], xfm[:, slot, h, :],
                                k_fm[:, kv, kb * KB + c * kch:kb * KB + (c + 1) * kch],
                                start=True, stop=True),
                                reads=[("xfm", slot)] + kf_reads, writes=[pk])
                        b.op("scalar", lambda e, it=it, h=h, kb=kb, p=p, ps_=ps_: e.activation(
                            probs[:, ps_, kb * KB:(kb + 1) * KB], p[:, 0:KB],
                            AF.Exp, bias=cq[:, it, h:h + 1], scale=1.0,
                            accum_out=sacc[:, ps_, kb:kb + 1]),
                            reads=[pk, ("cq", it)],
                            writes=[("probs", ps_), ("sacc", ps_, kb)])
                def c2_stage2(it, h):
                    kv = h // G
                    ps_ = h % 2
                    if NKB == 2:
                        b.op("vector", lambda e, h=h: e.tensor_tensor(
                            sums[:, h:h + 1], sacc[:, h % 2, 0:1], sacc[:, h % 2, 1:2],
                            ALU.add),
                            reads=[("sacc", h % 2, 0), ("sacc", h % 2, 1)],
                            writes=[("sums", h)])
                    else:
                        b.op("vector", lambda e, h=h: e.tensor_copy(
                            sums[:, h:h + 1], sacc[:, h % 2, 0:1]),
                            reads=[("sacc", h % 2, 0)], writes=[("sums", h)])
                    b.op("vector", lambda e, h=h: e.reciprocal(
                        rsum[:, h:h + 1], sums[:, h:h + 1]),
                        reads=[("sums", h)], writes=[("rsum", h)])
                    b.op("vector", lambda e, h=h, ps_=ps_: e.tensor_scalar(
                        xnbf[:, ps_, 0:T], probs[:, ps_, 0:T],
                        rsum[:, h:h + 1], None, ALU.mult),
                        reads=[("probs", ps_), ("rsum", h)], writes=[("xnbf", ps_)])
                    for c in range(TT):
                        b.op("tensor", lambda e, c=c, ps_=ps_: e.transpose(
                            pB[:, c * 128:(c + 1) * 128],
                            xnbf[:, ps_, c * 128:(c + 1) * 128], ident[:, :]),
                            reads=[("xnbf", ps_), ("ident",)], writes=[("pB",)])
                    half = T // 2
                    b.op("vector", lambda e, half=half, ps_=ps_: e.tensor_copy(
                        probsT[:, ps_, :, :].rearrange("p a b -> p (a b)")[:, 0:half],
                        pB[:, 0:half]),
                        reads=[("pB",)], writes=[("probsT", ps_)])
                    b.op("scalar", lambda e, half=half, ps_=ps_: e.activation(
                        probsT[:, ps_, :, :].rearrange("p a b -> p (a b)")[:, half:T],
                        pB[:, half:T], AF.Identity),
                        reads=[("pB",)], writes=[("probsT", ps_)])
                def c2_stage2b(it, h):
                    kv = h // G
                    ps_ = h % 2
                    pv, pvk = (pC0, ("pC0",)) if h % 2 == 0 else (pC1, ("pC1",))
                    for kt in range(TT):
                        b.op("tensor", lambda e, kt=kt, kv=kv, pv=pv, ps_=ps_: e.matmul(
                            pv[:, 0:128], v_tm[:, kt, kv * 128:(kv + 1) * 128],
                            probsT[:, ps_, kt, :],
                            start=(kt == 0), stop=(kt == TT - 1)),
                            reads=[("probsT", ps_)] + vt_reads, writes=[pvk])
                    b.op("vector", lambda e, h=h, pv=pv, pvk=pvk: e.tensor_copy(
                        encT[:, h, :], pv[:, 0:128]),
                        reads=[pvk], writes=[("encT", h)])

                enc_reads = [("encT", i) for i in range(NH)]

                for s in range(S):
                    b.dma("gpsimd", wbig[:, 0:KT, :], qkv_[:, :, :], writes=[("wbig",)])
                    b.dma("gpsimd", kvt[:, :, 0:KVW], kkv[:, :, :], writes=[("kvt",)])
                    a_stage1(s, 0)
                    for it in range(TT):
                        a_proj(s, it)
                        if it + 1 < TT:
                            a_stage1(s, it + 1)
                        a_rope(s, it)

                    b.dma("gpsimd", kvt[:, :, 0:KVW], vkv[:, :, :], writes=[("kvt",)])
                    a_stage1(s, 0)
                    for it in range(TT):
                        slot = it % 2
                        pv = pC0 if it % 2 == 0 else pC1
                        pvk = ("pC0",) if it % 2 == 0 else ("pC1",)
                        for kt in range(KT):
                            b.op("tensor", lambda e, kt=kt, slot=slot, pv=pv: e.matmul(
                                pv[:, 0:KVW], xfm[:, slot, kt, :], kvt[:, kt, 0:KVW],
                                start=(kt == 0), stop=(kt == KT - 1)),
                                reads=[("xfm", slot), ("kvt",)], writes=[pvk])
                        if it + 1 < TT:
                            a_stage1(s, it + 1)
                        b.op("vector", lambda e, it=it, pv=pv, pvk=pvk: e.tensor_copy(
                            v_tm[:, it, 0:KVW], pv[:, 0:KVW]),
                            reads=[pvk], writes=[("v_tm", it)])

                    b.dma("gpsimd", wbig[:, 0:NH, :], okv[:, :, :], writes=[("wbig",)])

                    c2_load(0)
                    for it in range(TT):
                        slot = it % 2
                        xsl = it % 3
                        tkey = ("xt", xsl)
                        b.dma("sync", xt[:, xsl, :], src[s, it * 128:(it + 1) * 128, :],
                              reads=[(src_name, s, it)], writes=[tkey])
                        if it + 1 < TT:
                            c2_load(it + 1)
                        for h in range(NH):
                            c2_stage1(it, h)
                            if h > 0:
                                c2_stage2(it, h - 1)
                            if h > 1:
                                c2_stage2b(it, h - 2)
                        c2_stage2(it, NH - 1)
                        c2_stage2b(it, NH - 2)
                        c2_stage2b(it, NH - 1)
                        for c in range(DC):
                            p, off, pk = PD(c)
                            for kt in range(NH):
                                b.op("tensor", lambda e, c=c, kt=kt, p=p, off=off: e.matmul(
                                    p[:, off:off + 512], encT[:, kt, :],
                                    wbig[:, kt, c * 512:(c + 1) * 512],
                                    start=(kt == 0), stop=(kt == NH - 1)),
                                    reads=[("wbig",)] + enc_reads, writes=[pk])
                        for c in range(DC):
                            p, off, pk = PD(c)
                            b.op("vector", lambda e, c=c, xsl=xsl, p=p, off=off: e.tensor_tensor(
                                u1[:, c * 512:(c + 1) * 512], p[:, off:off + 512],
                                xt[:, xsl, c * 512:(c + 1) * 512], ALU.add),
                                reads=[pk, tkey], writes=[("u1",)])
                        b.dma("scalar", dst[s, it * 128:(it + 1) * 128, :], u1[:, 0:D],
                              reads=[("u1",)], writes=[(dst_name, s, it)])
            b.barrier()

        ffn(x_in, "x", xs1, "xs1", prep=True)
        attn(xs1, "xs1", xs2, "xs2")
        ffn(xs2, "xs2", y_out, "y", prep=False)

        b.finish()
    return nc


def kernel(**inputs):
    cfg = cfg_full()
    n_cores = 8
    per = inputs["L0_attn_x"].shape[0] // n_cores
    nc = build_nc(cfg)
    namemap = {
        "qk": "L0_attn_q_kernel", "kk": "L0_attn_k_kernel", "vk": "L0_attn_v_kernel",
        "ok": "L0_attn_out_kernel", "ing": "L0_ffn_in_gamma", "inb": "L0_ffn_in_beta",
        "aggw": "L0_ffn_agg_w", "aggb": "L0_ffn_agg_b", "png": "L0_ffn_pn_gamma",
        "pnb": "L0_ffn_pn_beta", "d1w": "L0_ffn_d1_w", "d1b": "L0_ffn_d1_b",
        "d2w": "L0_ffn_d2_w", "d2b": "L0_ffn_d2_b", "bcw": "L0_ffn_bc_w",
        "bcb": "L0_ffn_bc_b", "gw": "L0_ffn_gate_w", "gb": "L0_ffn_gate_b",
        "outg": "L0_ffn_out_gamma", "outb": "L0_ffn_out_beta",
    }
    x = np.ascontiguousarray(np.asarray(inputs["L0_attn_x"], dtype=np.float32))
    weights = {k: np.ascontiguousarray(np.asarray(inputs[v], dtype=np.float32))
               for k, v in namemap.items()}
    in_maps = []
    for c in range(n_cores):
        m = {"x": x[c * per:(c + 1) * per]}
        m.update(weights)
        in_maps.append(m)
    res = run_bass_kernel_spmd(nc, in_maps, core_ids=list(range(n_cores)))
    return np.concatenate([np.asarray(r["y"]) for r in res.results], axis=0)


# version 24
# speedup vs baseline: 1.0550x; 1.0395x over previous
import math
from contextlib import ExitStack

import numpy as np

import concourse.bass as bass
import concourse.mybir as mybir
from concourse.bass_utils import run_bass_kernel_spmd

F32 = mybir.dt.float32
BF16 = mybir.dt.bfloat16
AF = mybir.ActivationFunctionType
ALU = mybir.AluOpType
AX = mybir.AxisListType.X

LN1E4 = math.log(10000.0)
LN1E4_HI = float(np.float32(LN1E4))
LN1E4_LO = float(LN1E4 - LN1E4_HI)
TWO_PI = 2.0 * math.pi
TWO_PI_HI = float(np.float32(TWO_PI))
TWO_PI_LO = float(TWO_PI - TWO_PI_HI)
PI_F = float(np.float32(math.pi))


def cfg_full():
    return dict(S=2, T=2048, D=2048, HF=8192, HD=128, NH=16, NKV=4, KSTEPS=4)


class B:
    ENGINES = ("tensor", "vector", "scalar", "gpsimd", "sync")

    def __init__(self, nc, st):
        self.nc = nc
        self.sem = {e: st.enter_context(nc.semaphore("S_" + e)) for e in self.ENGINES}
        self.cnt = {e: 0 for e in self.ENGINES}
        self.dmaq = {}
        for q in ("sync", "gpsimd", "scalar"):
            self.dmaq[q] = dict(
                sems=[st.enter_context(nc.semaphore(f"D_{q}_{i}")) for i in range(8)],
                tot=[0] * 8, i=0)
        self.bufs = {}
        self.waited = {e: {} for e in self.ENGINES}

    def _wait(self, eng, tok):
        if tok is None:
            return
        sem, val = tok
        d = self.waited[eng]
        k = id(sem)
        if d.get(k, 0) >= val:
            return
        getattr(self.nc, eng).wait_ge(sem, val)
        d[k] = val

    def _dep_waits(self, eng, reads, writes, skip_same=False):
        own = self.sem[eng]
        for bk in reads:
            e = self.bufs.get(bk)
            if e and e["w"] is not None:
                if skip_same and e["w"][0] is own:
                    continue
                self._wait(eng, e["w"])
        for bk in writes:
            e = self.bufs.get(bk)
            if e:
                if e["w"] is not None and not (skip_same and e["w"][0] is own):
                    self._wait(eng, e["w"])
                for t in e["r"]:
                    if skip_same and t[0] is own:
                        continue
                    self._wait(eng, t)

    def _record(self, tok, reads, writes):
        for bk in reads:
            self.bufs.setdefault(bk, dict(w=None, r=[]))["r"].append(tok)
        for bk in writes:
            self.bufs[bk] = dict(w=tok, r=[])

    def op(self, eng, emit, reads=(), writes=()):
        skip = eng == "tensor"
        self._dep_waits(eng, reads, writes, skip_same=skip)
        inst = emit(getattr(self.nc, eng))
        self.cnt[eng] += 1
        inst.then_inc(self.sem[eng], 1)
        tok = (self.sem[eng], self.cnt[eng])
        self._record(tok, reads, writes)
        return tok

    def dma(self, q, dst, src, reads=(), writes=()):
        Q = self.dmaq[q]
        i = Q["i"]
        Q["i"] = (i + 1) % len(Q["sems"])
        sem = Q["sems"][i]
        if Q["tot"][i]:
            self._wait(q, (sem, Q["tot"][i]))
        self._dep_waits(q, reads, writes)
        inst = getattr(self.nc, q).dma_start(dst, src)
        Q["tot"][i] += 16
        inst.then_inc(sem, 16)
        tok = (sem, Q["tot"][i])
        self._record(tok, reads, writes)
        return tok

    def barrier(self):
        toks = [(self.sem[e], self.cnt[e]) for e in self.ENGINES]
        for q, Q in self.dmaq.items():
            for i, sem in enumerate(Q["sems"]):
                if Q["tot"][i]:
                    toks.append((sem, Q["tot"][i]))
        for e in self.ENGINES:
            for t in toks:
                if t[0] is self.sem[e]:
                    continue
                self._wait(e, t)

    def finish(self):
        for q, Q in self.dmaq.items():
            for i, sem in enumerate(Q["sems"]):
                if Q["tot"][i]:
                    self._wait("sync", (sem, Q["tot"][i]))


def build_nc(cfg):
    S, T, D, HF, HD = cfg["S"], cfg["T"], cfg["D"], cfg["HF"], cfg["HD"]
    NH, NKV, KSTEPS = cfg["NH"], cfg["NKV"], cfg["KSTEPS"]
    G = NH // NKV
    KT = D // 128
    TT = T // 128
    KVW = NKV * HD
    assert KVW <= 512
    DC = D // min(512, D)
    CB = min(2048, HF)
    KB = min(1024, T)
    NKB = T // KB
    QSC = 1.0 / math.sqrt(HD)
    NEG_CQC = -1.02 * math.sqrt(KVW * 128.0 / HD)
    NHX = max(NH, 4)

    nc = bass.Bass("TRN2", target_bir_lowering=False)
    ap = lambda h: h.ap()

    x_in = ap(nc.dram_tensor("x", [S, T, D], F32, kind="ExternalInput"))
    y_out = ap(nc.dram_tensor("y", [S, T, D], F32, kind="ExternalOutput"))
    w = {}
    for name, shape in [
        ("qk", [D, NH * HD]), ("kk", [D, KVW]), ("vk", [D, KVW]), ("ok", [NH * HD, D]),
        ("ing", [D]), ("inb", [D]), ("aggw", [D, D]), ("aggb", [D]),
        ("png", [D]), ("pnb", [D]), ("d1w", [D, HF]), ("d1b", [HF]),
        ("d2w", [HF, D]), ("d2b", [D]), ("bcw", [D, D]), ("bcb", [D]),
        ("gw", [D, D]), ("gb", [D]), ("outg", [D]), ("outb", [D]),
    ]:
        w[name] = ap(nc.dram_tensor(name, shape, F32, kind="ExternalInput"))
    xs1 = ap(nc.dram_tensor("xs1", [S, T, D], F32, kind="Internal"))
    xs2 = ap(nc.dram_tensor("xs2", [S, T, D], F32, kind="Internal"))
    GKc = max(1, min(4, 16 // DC))
    NSLAB = (HF // 128) // GKc
    d1bf = ap(nc.dram_tensor("d1bf", [HF // 512, 128, KT * 512], BF16, kind="Internal"))
    d2bf = ap(nc.dram_tensor("d2bf", [NSLAB, 128, GKc * DC * 512], BF16, kind="Internal"))
    qrope = ap(nc.dram_tensor("qrope", [T, D], BF16, kind="Internal"))
    sgd = ap(nc.dram_tensor("sgd", [S, T, D], BF16, kind="Internal"))
    d1v = w["d1w"].rearrange("(a p) n -> p a n", p=128)
    d2v = w["d2w"].rearrange("(a p) n -> p a n", p=128)
    aggv = w["aggw"].rearrange("(a p) n -> p a n", p=128)
    bcv = w["bcw"].rearrange("(a p) n -> p a n", p=128)
    gwv = w["gw"].rearrange("(a p) n -> p a n", p=128)
    qkv_ = w["qk"].rearrange("(a p) n -> p a n", p=128)
    kkv = w["kk"].rearrange("(a p) n -> p a n", p=128)
    vkv = w["vk"].rearrange("(a p) n -> p a n", p=128)
    okv = w["ok"].rearrange("(a p) n -> p a n", p=128)

    with ExitStack() as st:
        b = B(nc, st)
        sb = lambda n, s, d=F32: st.enter_context(nc.sbuf_tensor(n, s, d))
        psb = lambda n, s, d=F32: st.enter_context(nc.psum_tensor(n, s, d))

        pD0 = psb("pD0", [128, 1024])
        pD1 = psb("pD1", [128, 1024])
        pB = psb("pB", [128, 2048], BF16)
        pC0 = psb("pC0", [128, 512])
        pC1 = psb("pC1", [128, 512])

        def PD(c):
            t = (pD0, ("pD0",)) if (c // 2) % 2 == 0 else (pD1, ("pD1",))
            return t[0], (c % 2) * 512, t[1]

        ident = sb("ident", [128, 128], BF16)
        ones1 = sb("ones1", [128, 128], BF16)
        onec = sb("onec", [128, 1])
        b.op("gpsimd", lambda e: e.memset(ident[:, :], 0.0), writes=[("ident",)])
        b.op("gpsimd", lambda e: e.affine_select(
            out=ident[:, :], in_=ident[:, :], compare_op=ALU.not_equal, fill=1.0,
            base=0, pattern=[[-1, 128]], channel_multiplier=1),
            reads=[("ident",)], writes=[("ident",)])
        b.op("gpsimd", lambda e: e.memset(ones1[:, :], 1.0), writes=[("ones1",)])
        b.op("gpsimd", lambda e: e.memset(onec[:, :], 1.0), writes=[("onec",)])

        xt = sb("xt", [128, 3, D])
        st6 = sb("st6", [128, 4, NHX, 6])
        sth2 = sb("sth2", [128, 4, NHX, 2])
        st2 = sb("st2", [128, 4, 2])
        mu = sb("mu", [128, 4])
        rr = sb("rr", [128, 4])
        nmr = sb("nmr", [128, 4])
        n1 = sb("n1", [128, 4])
        n2 = sb("n2", [128, 4])
        msh = sb("msh", [128, 4, NHX])
        xnbf = sb("xnbf", [128, 2, max(D, CB)], BF16)
        xfm = sb("xfm", [128, 2, KT, 128], BF16)
        u0 = sb("u0", [128, max(D, CB)])
        u1 = sb("u1", [128, max(D, CB)])
        wbig = sb("wbig", [128, max(KT, NH), D], BF16)

        def rsqrt_col(v_ap, v_key, sl, eps, rows=128):
            b.op("vector", lambda e: e.tensor_scalar(n1[0:rows, sl:sl + 1], v_ap, 1.0,
                                                     eps, ALU.mult, ALU.add),
                 reads=[v_key], writes=[("n1", sl)])
            b.op("scalar", lambda e: e.activation(n2[0:rows, sl:sl + 1],
                                                  n1[0:rows, sl:sl + 1], AF.Ln),
                 reads=[("n1", sl)], writes=[("n2", sl)])
            b.op("scalar", lambda e: e.activation(rr[0:rows, sl:sl + 1],
                                                  n2[0:rows, sl:sl + 1], AF.Exp,
                                                  scale=-0.5),
                 reads=[("n2", sl)], writes=[("rr", sl)])

        def ln_stats(src_ap, src_key, sl, eps, rows=128, neg_mu_r=False):
            chunk = min(512, D)
            nch = D // chunk
            for c in range(nch):
                b.op("vector", lambda e, c=c: e.bn_stats(
                    st6[0:rows, sl, c, :], src_ap[:, c * chunk:(c + 1) * chunk]),
                    reads=[src_key], writes=[("st6", sl)])
            b.op("vector", lambda e, nch=nch: e.bn_aggr(st2[0:rows, sl, :],
                                                        st6[0:rows, sl, 0:nch, :]),
                 reads=[("st6", sl)], writes=[("st2", sl)])
            b.op("vector", lambda e: e.tensor_copy(mu[0:rows, sl:sl + 1],
                                                   st2[0:rows, sl, 0:1]),
                 reads=[("st2", sl)], writes=[("mu", sl)])
            rsqrt_col(st2[0:rows, sl, 1:2], ("st2", sl), sl, eps, rows)
            if neg_mu_r:
                b.op("vector", lambda e: e.tensor_scalar(
                    nmr[0:rows, sl:sl + 1], mu[0:rows, sl:sl + 1],
                    rr[0:rows, sl:sl + 1], -1.0, ALU.mult, ALU.mult),
                    reads=[("mu", sl), ("rr", sl)], writes=[("nmr", sl)])

        def transpose_tile(src_ap, src_key, tslot, dst_ap=None, dst_key=None):
            for k in range(KT):
                b.op("tensor", lambda e, k=k: e.transpose(
                    pB[:, k * 128:(k + 1) * 128], src_ap[:, k * 128:(k + 1) * 128],
                    ident[:, :]),
                    reads=[src_key, ("ident",)], writes=[("pB",)])
            if dst_ap is None:
                dst_ap = xfm[:, tslot, :, :].rearrange("p a b -> p (a b)")[:, 0:D]
                dst_key = ("xfm", tslot)
            b.op("vector", lambda e, dst_ap=dst_ap, dst_key=dst_key: e.tensor_copy(
                dst_ap, pB[:, 0:D]),
                reads=[("pB",)], writes=[dst_key])

        def ffn(src, src_name, dst, dst_name, prep):
            tag = "_" + dst_name
            with ExitStack() as sc:
                sbl = lambda n, s, d=F32: sc.enter_context(nc.sbuf_tensor(n + tag, s, d))
                win = sbl("win", [128, 3, KT, 512], BF16)
                wbias = sbl("wbias", [1, 3, 512], BF16)
                usig = sbl("usig", [128, 2, D], BF16)
                sA = sbl("sA", [128, S * TT, 2])
                tfm = sbl("tfm", [128, KT, 2], BF16)
                t2fm = sbl("t2fm", [128, HF // 128, 2], BF16)
                png2 = sbl("png2", [2, D], BF16)
                pnb2 = sbl("pnb2", [2, D], BF16)
                brows = sbl("brows", [128, D], BF16)
                gOB = sbl("gOB", [128, D], BF16)
                bOB = sbl("bOB", [128, D], BF16)
                bcB = sbl("bcB", [128, max(D, CB)], BF16)
                vrow = win[0:1, 0, 0:(D // 512), :].rearrange("p a b -> p (a b)")
                th = u0
                ua = u1
                tnb = bcB[0:2, :]


                def bcast_row(dst_ap, dst_key, width, scale=None):
                    for c in range(width // 512):
                        p, off, pk = PD(c)
                        b.op("tensor", lambda e, c=c, p=p, off=off: e.matmul(
                            p[:, off:off + 512], ones1[0:1, :],
                            vrow[:, c * 512:(c + 1) * 512], start=True, stop=True),
                            reads=[("win", 0), ("ones1",)], writes=[pk])
                    for c in range(width // 512):
                        p, off, pk = PD(c)
                        if scale is None:
                            b.op("vector", lambda e, c=c, p=p, off=off: e.tensor_copy(
                                dst_ap[:, c * 512:(c + 1) * 512], p[:, off:off + 512]),
                                reads=[pk], writes=[dst_key])
                        else:
                            b.op("vector", lambda e, c=c, p=p, off=off: e.tensor_scalar(
                                dst_ap[:, c * 512:(c + 1) * 512], p[:, off:off + 512],
                                scale, None, ALU.mult),
                                reads=[pk], writes=[dst_key])

                def load_2row(dstt, dst_key, srcv, n):
                    b.dma("gpsimd", dstt[0:1, 0:n], srcv[None, 0:n], writes=[dst_key])
                    b.dma("gpsimd", dstt[1:2, 0:n], srcv[None, 0:n], writes=[dst_key])

                load_2row(png2, ("png2",), w["ing"], D)
                load_2row(pnb2, ("pnb2",), w["inb"], D)
                b.dma("gpsimd", brows[0:1, :], w["d2b"][None, :], writes=[("brows",)])
                b.dma("gpsimd", brows[32:33, :], w["aggb"][None, :], writes=[("brows",)])
                b.dma("gpsimd", brows[64:65, :], w["bcb"][None, :], writes=[("brows",)])
                b.dma("gpsimd", brows[96:97, :], w["gb"][None, :], writes=[("brows",)])

                def bias_mm(p, off, pk, row, cslice, m):
                    tp = (row, 0) if row else None
                    b.op("tensor", lambda e: e.matmul(
                        p[0:m, off:off + 512], ones1[row:row + 1, 0:m],
                        brows[row:row + 1, cslice:cslice + 512],
                        start=False, stop=True, tile_position=tp),
                        reads=[("brows",), ("ones1",)], writes=[pk])
                b.dma("gpsimd", wbig[:, 0:KT, :], gwv[:, :, :], writes=[("wbig",)])

                def tr2(src_ap, src_key, dst_ap, dst_key, ncols):
                    nk = ncols // 128
                    for k in range(nk):
                        b.op("tensor", lambda e, k=k: e.transpose(
                            pB[:, 2 * k:2 * k + 2],
                            src_ap[0:2, k * 128:(k + 1) * 128], ident[0:2, 0:2]),
                            reads=[src_key, ("ident",)], writes=[("pB",)])
                    b.op("vector", lambda e: e.tensor_copy(
                        dst_ap[:, 0:2 * nk], pB[:, 0:2 * nk]),
                        reads=[("pB",)], writes=[dst_key])

                def ln2row(state_ap, state_key, g2, g2k, b2, b2k, eps):
                    ln_stats(state_ap, state_key, 3, eps, rows=2)
                    b.op("vector", lambda e: e.tensor_scalar(
                        ua[0:2, 0:D], state_ap, mu[0:2, 3:4], rr[0:2, 3:4],
                        ALU.subtract, ALU.mult),
                        reads=[state_key, ("mu", 3), ("rr", 3)], writes=[("u1",)])
                    b.op("vector", lambda e: e.tensor_mul(ua[0:2, 0:D], ua[0:2, 0:D], g2),
                         reads=[("u1",), g2k], writes=[("u1",)])
                    b.op("vector", lambda e: e.tensor_tensor(tnb[0:2, 0:D], ua[0:2, 0:D],
                                                             b2, ALU.add),
                         reads=[("u1",), b2k], writes=[("bcB",)])

                def mm_cols(wview, brow_row, kt_total, n_out):
                    for cc in range(n_out // 512):
                        slot = cc % 3
                        b.dma("gpsimd", win[:, slot, 0:KT, :],
                              wview[:, :, cc * 512:(cc + 1) * 512],
                              writes=[("win", slot)])
                        p, off, pk = PD(cc)
                        for kt in range(kt_total):
                            b.op("tensor", lambda e, kt=kt, slot=slot, p=p, off=off: e.matmul(
                                p[0:2, off:off + 512], tfm[:, kt, :],
                                win[:, slot, kt, :],
                                start=(kt == 0), stop=False),
                                reads=[("tfm",), ("win", slot)], writes=[pk])
                        bias_mm(p, off, pk, brow_row, cc * 512, 2)
                        pump(1)

                def stageA1(s, it):
                    n = s * TT + it
                    slot = n % 3
                    xsl = slot
                    tkey = ("xt", xsl)
                    b.dma("sync", xt[:, xsl, :], src[s, it * 128:(it + 1) * 128, :],
                          reads=[(src_name, s, it)], writes=[tkey])
                    ln_stats(xt[:, xsl, :], tkey, slot, 1e-5, neg_mu_r=True)
                    b.op("vector", lambda e, n=n, slot=slot: e.tensor_copy(
                        sA[:, n, 0:1], nmr[:, slot:slot + 1]),
                        reads=[("nmr", slot)], writes=[("sA", n)])
                    b.op("vector", lambda e, n=n, slot=slot: e.tensor_copy(
                        sA[:, n, 1:2], rr[:, slot:slot + 1]),
                        reads=[("rr", slot)], writes=[("sA", n)])

                def stageA2(s, it):
                    n = s * TT + it
                    slot = n % 3
                    xsl = slot
                    fsl = n % 2
                    b.op("scalar", lambda e, slot=slot, xsl=xsl, fsl=fsl: e.activation(
                        usig[:, fsl, 0:D], xt[:, xsl, :], AF.Identity,
                        bias=nmr[:, slot:slot + 1], scale=rr[:, slot:slot + 1]),
                        reads=[("xt", xsl), ("nmr", slot), ("rr", slot)],
                        writes=[("usig", fsl)])
                    transpose_tile(usig[:, fsl, 0:D], ("usig", fsl), 0,
                                   dst_ap=xfm[:, fsl].rearrange("p a b -> p (a b)")[:, 0:D],
                                   dst_key=("xfm", fsl))

                PCs = [(pC0, ("pC0",)), (pC1, ("pC1",))]

                def stageB_chunk(s, it, c, fsl):
                    p, pk = PCs[c % 2]
                    for kt in range(KT):
                        b.op("tensor", lambda e, c=c, kt=kt, p=p, fsl=fsl: e.matmul(
                            p[:, 0:512], xfm[:, fsl, kt, :],
                            wbig[:, kt, c * 512:(c + 1) * 512],
                            start=(kt == 0), stop=False),
                            reads=[("xfm", fsl), ("wbig",)], writes=[pk])
                    bias_mm(p, 0, pk, 96, c * 512, 128)
                    b.op("scalar", lambda e, c=c, p=p, fsl=fsl: e.activation(
                        usig[:, fsl, c * 512:(c + 1) * 512], p[:, 0:512],
                        AF.Exp, scale=-1.0),
                        reads=[pk], writes=[("usig", fsl)])

                def sig_tail(s, it, fsl):
                    b.op("scalar", lambda e, fsl=fsl: e.activation(
                        usig[:, fsl, 0:D], usig[:, fsl, 0:D],
                        AF.Ln, bias=onec[:, 0:1]),
                        reads=[("usig", fsl), ("onec",)], writes=[("usig", fsl)])
                    b.op("scalar", lambda e, fsl=fsl: e.activation(
                        usig[:, fsl, 0:D], usig[:, fsl, 0:D],
                        AF.Exp, scale=-1.0),
                        reads=[("usig", fsl)], writes=[("usig", fsl)])
                    b.dma("gpsimd", sgd[s, it * 128:(it + 1) * 128, :],
                          usig[:, fsl, 0:D],
                          reads=[("usig", fsl)], writes=[("sgd", s, it)])

                def b1_gen():
                    tiles = [(s_, it_) for s_ in range(S) for it_ in range(TT)]
                    stageA1(*tiles[0])
                    yield
                    stageA2(*tiles[0])
                    yield
                    if len(tiles) > 1:
                        stageA1(*tiles[1])
                        yield
                    for n, (s_, it_) in enumerate(tiles):
                        for c in range(DC):
                            stageB_chunk(s_, it_, c, n % 2)
                            yield
                        if n + 2 < len(tiles):
                            stageA1(*tiles[n + 2])
                            yield
                        sig_tail(s_, it_, n % 2)
                        yield
                        if n + 1 < len(tiles):
                            stageA2(*tiles[n + 1])
                            yield

                gen = b1_gen()
                gen_done = [False]

                def pump(k=1):
                    for _ in range(k):
                        if gen_done[0]:
                            return
                        try:
                            next(gen)
                        except StopIteration:
                            gen_done[0] = True
                            return

                for s in range(S):
                    b.dma("sync", ua[s:s + 1, 0:D], src[s, T - 1:T, :],
                          reads=[(src_name, s, TT - 1)], writes=[("u1",)])
                ln_stats(ua[0:2, 0:D], ("u1",), 3, 1e-5, rows=2)
                b.op("vector", lambda e: e.tensor_scalar(ua[0:2, 0:D], ua[0:2, 0:D],
                                                         mu[0:2, 3:4], rr[0:2, 3:4],
                                                         ALU.subtract, ALU.mult),
                     reads=[("u1",), ("mu", 3), ("rr", 3)], writes=[("u1",)])
                ln_stats(ua[0:2, 0:D], ("u1",), 3, 1e-6, rows=2)
                b.op("vector", lambda e: e.tensor_scalar(ua[0:2, 0:D], ua[0:2, 0:D],
                                                         mu[0:2, 3:4], rr[0:2, 3:4],
                                                         ALU.subtract, ALU.mult),
                     reads=[("u1",), ("mu", 3), ("rr", 3)], writes=[("u1",)])
                b.op("vector", lambda e: e.tensor_mul(ua[0:2, 0:D], ua[0:2, 0:D],
                                                      png2[0:2, 0:D]),
                     reads=[("u1",), ("png2",)], writes=[("u1",)])
                b.op("vector", lambda e: e.tensor_tensor(tnb[0:2, 0:D], ua[0:2, 0:D],
                                                         pnb2[0:2, 0:D], ALU.add),
                     reads=[("u1",), ("pnb2",)], writes=[("bcB",)])
                load_2row(png2, ("png2",), w["png"], D)
                load_2row(pnb2, ("pnb2",), w["pnb"], D)
                tr2(tnb, ("bcB",), tfm.rearrange("p a b -> p (a b)"), ("tfm",), D)
                mm_cols(aggv, 32, KT, D)
                for c in range(DC):
                    p, off, pk = PD(c)
                    b.op("vector", lambda e, c=c, p=p, off=off: e.tensor_copy(
                        th[0:2, c * 512:(c + 1) * 512], p[0:2, off:off + 512]),
                        reads=[pk], writes=[("u0",)])

                for ks in range(KSTEPS):
                    ln2row(th[0:2, 0:D], ("u0",), png2[0:2, 0:D], ("png2",),
                           pnb2[0:2, 0:D], ("pnb2",), 1e-6)
                    tr2(tnb, ("bcB",), tfm.rearrange("p a b -> p (a b)"), ("tfm",), D)
                    slot = 0
                    for cb0 in range(0, HF, CB):
                        for cc in range(CB // 512):
                            c0 = cb0 + cc * 512
                            slot = (cb0 // 512 + cc) % 3
                            b.dma("gpsimd", wbias[0:1, slot, 0:512],
                                  w["d1b"][None, c0:c0 + 512], writes=[("wbias", slot)])
                            if prep and ks == 0:
                                b.dma("gpsimd", win[:, slot, 0:KT, :],
                                      d1v[:, :, c0:c0 + 512], writes=[("win", slot)])
                            else:
                                b.dma("sync" if cc % 2 == 0 else "scalar",
                                      win[:, slot, 0:KT, :],
                                      d1bf[c0 // 512].rearrange("p (a n) -> p a n", n=512),
                                      reads=[("d1bf", c0)], writes=[("win", slot)])
                            p, off, pk = PD(cc)
                            for kt in range(KT):
                                b.op("tensor", lambda e, kt=kt, slot=slot, p=p, off=off: e.matmul(
                                    p[0:2, off:off + 512], tfm[:, kt, :],
                                    win[:, slot, kt, :],
                                    start=(kt == 0), stop=False),
                                    reads=[("tfm",), ("win", slot)], writes=[pk])
                            b.op("tensor", lambda e, p=p, off=off, slot=slot: e.matmul(
                                p[0:2, off:off + 512], ones1[0:1, 0:2],
                                wbias[0:1, slot, 0:512],
                                start=False, stop=True),
                                reads=[("wbias", slot)], writes=[pk])
                            if prep and ks == 0:
                                b.dma("scalar",
                                      d1bf[c0 // 512].rearrange("p (a n) -> p a n", n=512),
                                      win[:, slot, 0:KT, :],
                                      reads=[("win", slot)], writes=[("d1bf", c0)])
                            pump(1)
                        for half in range(CB // 1024):
                            p = pD0 if half == 0 else pD1
                            pk = ("pD0",) if half == 0 else ("pD1",)
                            b.op("scalar", lambda e, half=half, p=p: e.activation(
                                tnb[0:2, half * 1024:(half + 1) * 1024],
                                p[0:2, 0:1024], AF.Gelu),
                                reads=[pk], writes=[("bcB",)])
                        tr2(tnb[0:2, 0:CB], ("bcB",),
                            t2fm.rearrange("p a b -> p (a b)")[:, 2 * cb0 // 128:],
                            ("t2fm",), CB)
                        pump(1)
                    nk2 = HF // 128
                    GK = max(1, min(4, 16 // DC))
                    for kt0 in range(0, nk2, GK):
                        slot2 = (kt0 // GK) % 3
                        gsz = min(GK, nk2 - kt0)
                        slab = win[:, slot2, 0:gsz * DC, :].rearrange(
                            "p (a c) n -> p a c n", c=DC)
                        dview_w = w["d2w"][kt0 * 128:(kt0 + gsz) * 128, :].rearrange(
                            "(a p) (c n) -> p a c n", p=128, n=512)
                        dview_b = d2bf[kt0 // GK].rearrange(
                            "p (a c n) -> p a c n", c=DC, n=512)
                        if prep and ks == 0:
                            b.dma("gpsimd", slab, dview_w, writes=[("win", slot2)])
                            b.dma("scalar", dview_b, slab,
                                  reads=[("win", slot2)], writes=[("d2bf", kt0)])
                        else:
                            b.dma("sync" if (kt0 // GK) % 2 == 0 else "scalar", slab,
                                  dview_b,
                                  reads=[("d2bf", kt0)], writes=[("win", slot2)])
                        pump(1)
                        for g in range(gsz):
                            kt = kt0 + g
                            for cc in range(DC):
                                p, off, pk = PD(cc)
                                b.op("tensor", lambda e, kt=kt, g=g, cc=cc, slot2=slot2, p=p, off=off: e.matmul(
                                    p[0:2, off:off + 512], t2fm[:, kt, :],
                                    win[:, slot2, g * DC + cc, :],
                                    start=(kt == 0), stop=False),
                                    reads=[("t2fm",), ("win", slot2)], writes=[pk])
                        pump(1)
                    for cc in range(DC):
                        p, off, pk = PD(cc)
                        bias_mm(p, off, pk, 0, cc * 512, 2)
                    for cc in range(DC):
                        p, off, pk = PD(cc)
                        b.op("vector", lambda e, cc=cc, p=p, off=off: e.tensor_tensor(
                            th[0:2, cc * 512:(cc + 1) * 512],
                            th[0:2, cc * 512:(cc + 1) * 512],
                            p[0:2, off:off + 512], ALU.add),
                            reads=[("u0",), pk], writes=[("u0",)])

                b.op("vector", lambda e: e.tensor_copy(tnb[0:2, 0:D], th[0:2, 0:D]),
                     reads=[("u0",)], writes=[("bcB",)])
                tr2(tnb, ("bcB",), tfm.rearrange("p a b -> p (a b)"), ("tfm",), D)
                mm_cols(bcv, 64, KT, D)
                for c in range(DC):
                    p, off, pk = PD(c)
                    b.op("vector", lambda e, c=c, p=p, off=off: e.tensor_copy(
                        ua[0:2, c * 512:(c + 1) * 512], p[0:2, off:off + 512]),
                        reads=[pk], writes=[("u1",)])
                b.op("vector", lambda e: e.tensor_copy(brows[32:34, 0:D], ua[0:2, 0:D]),
                     reads=[("u1",)], writes=[("brows",)])

                while not gen_done[0]:
                    pump(1)
                b.dma("gpsimd", vrow[:, 0:D], w["outg"][None, :], writes=[("win", 0)])
                bcast_row(gOB, ("gOB",), D, scale=0.5)
                b.dma("gpsimd", vrow[:, 0:D], w["outb"][None, :], writes=[("win", 0)])
                bcast_row(bOB, ("bOB",), D, scale=0.5)

                def b2A(s, it):
                    xsl = it % 3
                    tkey = ("xt", xsl)
                    b.dma("sync", xt[:, xsl, :], src[s, it * 128:(it + 1) * 128, :],
                          reads=[(src_name, s, it)], writes=[tkey])
                    b.dma("scalar", xnbf[:, it % 2, 0:D], sgd[s, it * 128:(it + 1) * 128, :],
                          reads=[("sgd", s, it)], writes=[("xnbf", it % 2)])

                def b2C(s, it):
                    n = s * TT + it
                    slot = it % 3
                    xsl = it % 3
                    ut, uk = (u0, ("u0",)) if n % 2 == 0 else (u1, ("u1",))
                    b.op("scalar", lambda e, n=n, xsl=xsl, ut=ut: e.activation(
                        ut[:, 0:D], xt[:, xsl, :], AF.Identity,
                        bias=sA[:, n, 0:1], scale=sA[:, n, 1:2]),
                        reads=[("xt", xsl), ("sA", n)],
                        writes=[uk])
                    b.op("vector", lambda e, it=it: e.tensor_mul(
                        xnbf[:, it % 2, 0:D], xnbf[:, it % 2, 0:D], bcB[:, 0:D]),
                        reads=[("xnbf", it % 2), ("bcB",)], writes=[("xnbf", it % 2)])
                    b.op("vector", lambda e, it=it, ut=ut: e.tensor_tensor(
                        ut[:, 0:D], ut[:, 0:D], xnbf[:, it % 2, 0:D], ALU.add),
                        reads=[uk, ("xnbf", it % 2)], writes=[uk])
                    ln_stats(ut[:, 0:D], uk, slot, 1e-6, neg_mu_r=True)
                    fsl = n % 2
                    b.op("scalar", lambda e, slot=slot, ut=ut, fsl=fsl: e.activation(
                        usig[:, fsl, 0:D], ut[:, 0:D], AF.Identity,
                        bias=nmr[:, slot:slot + 1], scale=rr[:, slot:slot + 1]),
                        reads=[uk, ("nmr", slot), ("rr", slot)], writes=[("usig", fsl)])
                    b.op("vector", lambda e, fsl=fsl: e.tensor_mul(
                        usig[:, fsl, 0:D], usig[:, fsl, 0:D], gOB[:, 0:D]),
                        reads=[("usig", fsl), ("gOB",)], writes=[("usig", fsl)])
                    b.op("vector", lambda e, fsl=fsl: e.tensor_tensor(
                        usig[:, fsl, 0:D], usig[:, fsl, 0:D], bOB[:, 0:D], ALU.add),
                        reads=[("usig", fsl), ("bOB",)], writes=[("usig", fsl)])
                    b.op("vector", lambda e, ut=ut, fsl=fsl, xsl=xsl: e.tensor_tensor(
                        ut[:, 0:D], usig[:, fsl, 0:D], xt[:, xsl, :], ALU.add),
                        reads=[("usig", fsl), ("xt", xsl), uk], writes=[uk])
                    b.dma("scalar", dst[s, it * 128:(it + 1) * 128, :], ut[:, 0:D],
                          reads=[uk], writes=[(dst_name, s, it)])

                for s_ in range(S):
                    b.dma("gpsimd", vrow[:, 0:D], brows[32 + s_:33 + s_, 0:D],
                          reads=[("brows",)], writes=[("win", 0)])
                    bcast_row(bcB, ("bcB",), D)
                    b2A(s_, 0)
                    b2A(s_, 1)
                    for it_ in range(TT):
                        b2C(s_, it_)
                        if it_ + 2 < TT:
                            b2A(s_, it_ + 2)
            b.barrier()

        def attn(src, src_name, dst, dst_name):
            with ExitStack() as sc:
                sbl = lambda n, s, d=F32: sc.enter_context(nc.sbuf_tensor(n + "_at", s, d))
                cosTM = sbl("cosTM", [128, T], BF16)
                sinTM = sbl("sinTM", [128, T], BF16)
                with ExitStack() as st_init:
                    sbi = lambda n, s, d=F32: st_init.enter_context(nc.sbuf_tensor(n, s, d))
                    trow = sbi("trow", [1, T])
                    irow = sbi("irow", [1, HD])
                    rtmp = sbi("rtmp", [1, HD])
                    wr0 = sbi("wr0", [128, 128])
                    wr1 = sbi("wr1", [128, 128])
                    wr2 = sbi("wr2", [128, 128])
                    wr3 = sbi("wr3", [128, 128])
                    b.op("gpsimd", lambda e: e.iota(trow[:, :], pattern=[[1, T]], base=0,
                                                    channel_multiplier=0,
                                                    allow_small_or_imprecise_dtypes=True),
                         writes=[("trow",)])
                    b.op("gpsimd", lambda e: e.iota(irow[:, :], pattern=[[1, HD]], base=0,
                                                    channel_multiplier=0,
                                                    allow_small_or_imprecise_dtypes=True),
                         writes=[("irow",)])
                    b.op("vector", lambda e: e.tensor_single_scalar(rtmp[:, :], irow[:, :],
                                                                    64.0, ALU.is_ge),
                         reads=[("irow",)], writes=[("rtmp",)])
                    b.op("vector", lambda e: e.scalar_tensor_tensor(irow[:, :], rtmp[:, :], -64.0,
                                                                    irow[:, :], ALU.mult, ALU.add),
                         reads=[("rtmp",), ("irow",)], writes=[("irow",)])
                    b.op("vector", lambda e: e.tensor_scalar(rtmp[:, :], irow[:, :],
                                                             -LN1E4_LO / 64.0, None, ALU.mult),
                         reads=[("irow",)], writes=[("rtmp",)])
                    b.op("vector", lambda e: e.scalar_tensor_tensor(irow[:, :], irow[:, :],
                                                                    -LN1E4_HI / 64.0, rtmp[:, :],
                                                                    ALU.mult, ALU.add),
                         reads=[("irow",), ("rtmp",)], writes=[("irow",)])
                    b.op("scalar", lambda e: e.activation(irow[:, :], irow[:, :], AF.Exp),
                         reads=[("irow",)], writes=[("irow",)])
                    nstages = max(1, int(math.ceil(math.log2(max(2.0, T / TWO_PI)))))
                    for i in range(TT):
                        b.op("tensor", lambda e, i=i: e.matmul(
                            pD0[:, 0:128], trow[:, i * 128:(i + 1) * 128], irow[:, :],
                            start=True, stop=True),
                            reads=[("trow",), ("irow",)], writes=[("pD0",)])
                        b.op("vector", lambda e: e.tensor_copy(wr1[:, :], pD0[:, 0:128]),
                             reads=[("pD0",)], writes=[("wr1",)])
                        b.op("vector", lambda e: e.memset(wr2[:, :], 0.0), writes=[("wr2",)])
                        for j in range(nstages - 1, -1, -1):
                            cj = TWO_PI_HI * float(2 ** j)
                            b.op("vector", lambda e, cj=cj: e.tensor_single_scalar(
                                wr3[:, :], wr1[:, :], cj, ALU.is_ge),
                                reads=[("wr1",)], writes=[("wr3",)])
                            b.op("vector", lambda e, cj=cj: e.scalar_tensor_tensor(
                                wr1[:, :], wr3[:, :], -cj, wr1[:, :], ALU.mult, ALU.add),
                                reads=[("wr3",), ("wr1",)], writes=[("wr1",)])
                            b.op("vector", lambda e, j=j: e.scalar_tensor_tensor(
                                wr2[:, :], wr3[:, :], float(2 ** j), wr2[:, :],
                                ALU.mult, ALU.add),
                                reads=[("wr3",), ("wr2",)], writes=[("wr2",)])
                        b.op("vector", lambda e: e.scalar_tensor_tensor(
                            wr1[:, :], wr2[:, :], -TWO_PI_LO, wr1[:, :], ALU.mult, ALU.add),
                            reads=[("wr2",), ("wr1",)], writes=[("wr1",)])
                        b.op("vector", lambda e: e.tensor_single_scalar(
                            wr3[:, :], wr1[:, :], PI_F, ALU.is_ge),
                            reads=[("wr1",)], writes=[("wr3",)])
                        b.op("vector", lambda e: e.scalar_tensor_tensor(
                            wr0[:, :], wr3[:, :], -TWO_PI_HI, wr1[:, :], ALU.mult, ALU.add),
                            reads=[("wr3",), ("wr1",)], writes=[("wr0",)])
                        b.op("scalar", lambda e, i=i: e.activation(
                            sinTM[:, i * 128:(i + 1) * 128], wr0[:, :], AF.Sin),
                            reads=[("wr0",)], writes=[("sinTM",)])
                        b.op("vector", lambda e, i=i: e.tensor_scalar(
                            sinTM[:, i * 128:i * 128 + 64], sinTM[:, i * 128:i * 128 + 64],
                            -1.0, None, ALU.mult),
                            reads=[("sinTM",)], writes=[("sinTM",)])
                        b.op("vector", lambda e: e.tensor_scalar(
                            wr0[:, :], wr1[:, :], 1.0, PI_F / 2.0, ALU.mult, ALU.add),
                            reads=[("wr1",)], writes=[("wr0",)])
                        b.op("vector", lambda e: e.tensor_single_scalar(
                            wr3[:, :], wr0[:, :], PI_F, ALU.is_ge),
                            reads=[("wr0",)], writes=[("wr3",)])
                        b.op("vector", lambda e: e.scalar_tensor_tensor(
                            wr0[:, :], wr3[:, :], -TWO_PI_HI, wr0[:, :], ALU.mult, ALU.add),
                            reads=[("wr3",), ("wr0",)], writes=[("wr0",)])
                        b.op("scalar", lambda e, i=i: e.activation(
                            cosTM[:, i * 128:(i + 1) * 128], wr0[:, :], AF.Sin),
                            reads=[("wr0",)], writes=[("cosTM",)])
                    b.barrier()

                kvt = sbl("kvt", [128, KT, KVW], BF16)
                qbuf = sbl("qbuf", [128, 2, D], BF16)
                k_fm = sbl("k_fm", [128, NKV, T], BF16)
                v_tm = sbl("v_tm", [128, TT, KVW], BF16)
                cq = sbl("cq", [128, TT, NH], BF16)
                sums = sbl("sums", [128, NH])
                sacc = sbl("sacc", [128, 2, 2])
                rsum = sbl("rsum", [128, NH])
                probs = sbl("probs", [128, 2, max(T, D)], BF16)
                qsb = probs[:, 0, 0:D]
                ksb = probs[:, 1, 0:KVW]
                probsT = sbl("probsT", [128, 2, TT, 128], BF16)
                encT = sbl("encT", [128, NH, 128], BF16)
                dump = encT[:, 0, :]

                def rms_rope(src_sb, src_sb_key, width, ctile, stile,
                             out_ap, out_key, extra_scale, want_cq, it, sl):
                    nch = width // 128
                    for h in range(nch):
                        b.op("scalar", lambda e, h=h: e.activation(
                            dump[:, 0:128], src_sb[:, h * 128:(h + 1) * 128],
                            AF.Square, accum_out=msh[:, sl, h:h + 1]),
                            reads=[src_sb_key], writes=[("encT", 0), ("msh", sl)])
                    b.op("vector", lambda e, nch=nch: e.tensor_reduce(
                        st2[:, sl, 0:1], msh[:, sl, 0:nch], AX, ALU.add),
                        reads=[("msh", sl)], writes=[("st2", sl)])
                    b.op("vector", lambda e, nch=nch: e.tensor_scalar(
                        st2[:, sl, 0:1], st2[:, sl, 0:1], 1.0 / width, None, ALU.mult),
                        reads=[("st2", sl)], writes=[("st2", sl)])
                    rsqrt_col(st2[:, sl, 0:1], ("st2", sl), sl, 1e-6)
                    srcv = src_sb.rearrange("p (a n) -> p a n", n=128)
                    t0 = probsT[:, 0, :, :].rearrange("p a b -> p (a b)")[:, 0:width]
                    t1 = probsT[:, 1, :, :].rearrange("p a b -> p (a b)")[:, 0:width]
                    t0v = t0.rearrange("p (a n) -> p a n", n=128)
                    t1v = t1.rearrange("p (a n) -> p a n", n=128)
                    cb_ = ctile.unsqueeze(1).broadcast_to([128, nch, 128])
                    sb0 = stile[:, 0:64].unsqueeze(1).broadcast_to([128, nch, 64])
                    sb1 = stile[:, 64:128].unsqueeze(1).broadcast_to([128, nch, 64])
                    b.op("vector", lambda e, t0v=t0v: e.tensor_tensor(
                        t0v, srcv, cb_, ALU.mult),
                        reads=[src_sb_key, ("cosTM",)], writes=[("probsT", 0)])
                    b.op("vector", lambda e, t1v=t1v: e.tensor_tensor(
                        t1v[:, :, 0:64], srcv[:, :, 64:128], sb0, ALU.mult),
                        reads=[src_sb_key, ("sinTM",)], writes=[("probsT", 1)])
                    b.op("vector", lambda e, t1v=t1v: e.tensor_tensor(
                        t1v[:, :, 64:128], srcv[:, :, 0:64], sb1, ALU.mult),
                        reads=[src_sb_key, ("sinTM",)], writes=[("probsT", 1)])
                    b.op("vector", lambda e, t0=t0, t1=t1: e.tensor_tensor(
                        t0, t0, t1, ALU.add),
                        reads=[("probsT", 0), ("probsT", 1)], writes=[("probsT", 0)])
                    b.op("vector", lambda e, t0=t0: e.tensor_scalar(
                        out_ap, t0, rr[:, sl:sl + 1], extra_scale,
                        ALU.mult, ALU.mult),
                        reads=[("probsT", 0), ("rr", sl)], writes=[out_key])
                    if want_cq:
                        b.op("scalar", lambda e, nch=nch: e.activation(
                            msh[:, sl, 0:nch], msh[:, sl, 0:nch], AF.Ln),
                            reads=[("msh", sl)], writes=[("msh", sl)])
                        b.op("scalar", lambda e, nch=nch: e.activation(
                            msh[:, sl, 0:nch], msh[:, sl, 0:nch], AF.Exp, scale=0.5),
                            reads=[("msh", sl)], writes=[("msh", sl)])
                        b.op("vector", lambda e, nch=nch: e.tensor_scalar(
                            msh[:, sl, 0:nch], msh[:, sl, 0:nch], rr[:, sl:sl + 1],
                            NEG_CQC / math.sqrt(HD), ALU.mult, ALU.mult),
                            reads=[("msh", sl), ("rr", sl)], writes=[("msh", sl)])
                        b.op("vector", lambda e, it=it, nch=nch: e.tensor_copy(
                            cq[:, it, 0:nch], msh[:, sl, 0:nch]),
                            reads=[("msh", sl)], writes=[("cq", it)])

                def a_stage1(s, it):
                    slot = it % 2
                    xsl = it % 3
                    tkey = ("xt", xsl)
                    b.dma("sync", xt[:, xsl, :], src[s, it * 128:(it + 1) * 128, :],
                          reads=[(src_name, s, it)], writes=[tkey])
                    ln_stats(xt[:, xsl, :], tkey, slot, 1e-5, neg_mu_r=True)
                    b.op("scalar", lambda e, slot=slot, xsl=xsl: e.activation(
                        xnbf[:, slot, 0:D], xt[:, xsl, :], AF.Identity,
                        bias=nmr[:, slot:slot + 1], scale=rr[:, slot:slot + 1]),
                        reads=[tkey, ("nmr", slot), ("rr", slot)],
                        writes=[("xnbf", slot)])
                    transpose_tile(xnbf[:, slot, 0:D], ("xnbf", slot), slot)

                def a_proj(s, it):
                    slot = it % 2
                    for c in range(DC):
                        p, off, pk = PD(c)
                        for kt in range(KT):
                            b.op("tensor", lambda e, c=c, kt=kt, p=p, off=off, slot=slot: e.matmul(
                                p[:, off:off + 512], xfm[:, slot, kt, :],
                                wbig[:, kt, c * 512:(c + 1) * 512],
                                start=(kt == 0), stop=(kt == KT - 1)),
                                reads=[("xfm", slot), ("wbig",)], writes=[pk])
                    for kt in range(KT):
                        b.op("tensor", lambda e, kt=kt, slot=slot: e.matmul(
                            pC0[:, 0:KVW], xfm[:, slot, kt, :], kvt[:, kt, 0:KVW],
                            start=(kt == 0), stop=(kt == KT - 1)),
                            reads=[("xfm", slot), ("kvt",)], writes=[("pC0",)])

                def a_rope(s, it):
                    slot = it % 2
                    for c in range(DC):
                        p, off, pk = PD(c)
                        b.op("vector", lambda e, c=c, p=p, off=off: e.tensor_copy(
                            qsb[:, c * 512:(c + 1) * 512], p[:, off:off + 512]),
                            reads=[pk], writes=[("probs", 0)])
                    b.op("vector", lambda e: e.tensor_copy(ksb[:, 0:KVW], pC0[:, 0:KVW]),
                         reads=[("pC0",)], writes=[("probs", 1)])
                    ctile = cosTM[:, it * 128:(it + 1) * 128]
                    stile = sinTM[:, it * 128:(it + 1) * 128]
                    rms_rope(qsb, ("probs", 0), D, ctile, stile,
                             qbuf[:, slot, :], ("qbuf", slot), QSC, True, it, slot)
                    for k in range(KT):
                        b.op("tensor", lambda e, k=k, slot=slot: e.transpose(
                            pB[:, k * 128:(k + 1) * 128],
                            qbuf[:, slot, k * 128:(k + 1) * 128], ident[:, :]),
                            reads=[("qbuf", slot), ("ident",)], writes=[("pB",)])
                    b.op("vector", lambda e, slot=slot: e.tensor_copy(
                        qbuf[:, slot, 0:D], pB[:, 0:D]),
                        reads=[("pB",)], writes=[("qbuf", slot)])
                    b.dma("scalar", qrope[it * 128:(it + 1) * 128, :],
                          qbuf[:, slot, :], reads=[("qbuf", slot)],
                          writes=[("qrope", it)])
                    rms_rope(ksb, ("probs", 1), KVW, ctile, stile,
                             xnbf[:, slot, 0:KVW], ("xnbf", slot), 1.0, False, it, slot)
                    for h in range(NKV):
                        b.op("tensor", lambda e, h=h, slot=slot: e.transpose(
                            pB[:, h * 128:(h + 1) * 128],
                            xnbf[:, slot, h * 128:(h + 1) * 128], ident[:, :]),
                            reads=[("xnbf", slot), ("ident",)], writes=[("pB",)])
                    b.op("vector", lambda e, it=it: e.tensor_copy(
                        k_fm[:, 0:NKV, it * 128:(it + 1) * 128],
                        pB[:, 0:NKV * 128].rearrange("p (a b) -> p a b", b=128)),
                        reads=[("pB",)], writes=[("k_fm", it)])

                kf_reads = [("k_fm", i) for i in range(TT)]
                vt_reads = [("v_tm", i) for i in range(TT)]

                def c2_load(it):
                    slot = it % 2
                    b.dma("sync",
                          xfm[:, slot, :, :].rearrange("p a b -> p (a b)")[:, 0:D],
                          qrope[it * 128:(it + 1) * 128, :],
                          reads=[("qrope", it)], writes=[("xfm", slot)])

                def c2_stage1(it, h):
                    slot = it % 2
                    kv = h // G
                    ps_ = h % 2
                    for kb in range(NKB):
                        p = pD0 if (kb + NKB * h) % 2 == 0 else pD1
                        pk = ("pD0",) if (kb + NKB * h) % 2 == 0 else ("pD1",)
                        kch = min(512, KB)
                        for c in range(KB // kch):
                            b.op("tensor", lambda e, h=h, kb=kb, c=c, kv=kv, p=p, kch=kch, slot=slot: e.matmul(
                                p[:, c * kch:(c + 1) * kch], xfm[:, slot, h, :],
                                k_fm[:, kv, kb * KB + c * kch:kb * KB + (c + 1) * kch],
                                start=True, stop=True),
                                reads=[("xfm", slot)] + kf_reads, writes=[pk])
                        b.op("scalar", lambda e, it=it, h=h, kb=kb, p=p, ps_=ps_: e.activation(
                            probs[:, ps_, kb * KB:(kb + 1) * KB], p[:, 0:KB],
                            AF.Exp, bias=cq[:, it, h:h + 1], scale=1.0,
                            accum_out=sacc[:, ps_, kb:kb + 1]),
                            reads=[pk, ("cq", it)],
                            writes=[("probs", ps_), ("sacc", ps_, kb)])
                def c2_stage2(it, h):
                    kv = h // G
                    ps_ = h % 2
                    if NKB == 2:
                        b.op("vector", lambda e, h=h: e.tensor_tensor(
                            sums[:, h:h + 1], sacc[:, h % 2, 0:1], sacc[:, h % 2, 1:2],
                            ALU.add),
                            reads=[("sacc", h % 2, 0), ("sacc", h % 2, 1)],
                            writes=[("sums", h)])
                    else:
                        b.op("vector", lambda e, h=h: e.tensor_copy(
                            sums[:, h:h + 1], sacc[:, h % 2, 0:1]),
                            reads=[("sacc", h % 2, 0)], writes=[("sums", h)])
                    b.op("vector", lambda e, h=h: e.reciprocal(
                        rsum[:, h:h + 1], sums[:, h:h + 1]),
                        reads=[("sums", h)], writes=[("rsum", h)])
                    b.op("vector", lambda e, h=h, ps_=ps_: e.tensor_scalar(
                        xnbf[:, ps_, 0:T], probs[:, ps_, 0:T],
                        rsum[:, h:h + 1], None, ALU.mult),
                        reads=[("probs", ps_), ("rsum", h)], writes=[("xnbf", ps_)])
                    for c in range(TT):
                        b.op("tensor", lambda e, c=c, ps_=ps_: e.transpose(
                            pB[:, c * 128:(c + 1) * 128],
                            xnbf[:, ps_, c * 128:(c + 1) * 128], ident[:, :]),
                            reads=[("xnbf", ps_), ("ident",)], writes=[("pB",)])
                    half = T // 2
                    b.op("vector", lambda e, half=half, ps_=ps_: e.tensor_copy(
                        probsT[:, ps_, :, :].rearrange("p a b -> p (a b)")[:, 0:half],
                        pB[:, 0:half]),
                        reads=[("pB",)], writes=[("probsT", ps_)])
                    b.op("scalar", lambda e, half=half, ps_=ps_: e.activation(
                        probsT[:, ps_, :, :].rearrange("p a b -> p (a b)")[:, half:T],
                        pB[:, half:T], AF.Identity),
                        reads=[("pB",)], writes=[("probsT", ps_)])
                def c2_stage2b(it, h):
                    kv = h // G
                    ps_ = h % 2
                    pv, pvk = (pC0, ("pC0",)) if h % 2 == 0 else (pC1, ("pC1",))
                    for kt in range(TT):
                        b.op("tensor", lambda e, kt=kt, kv=kv, pv=pv, ps_=ps_: e.matmul(
                            pv[:, 0:128], v_tm[:, kt, kv * 128:(kv + 1) * 128],
                            probsT[:, ps_, kt, :],
                            start=(kt == 0), stop=(kt == TT - 1)),
                            reads=[("probsT", ps_)] + vt_reads, writes=[pvk])
                    b.op("vector", lambda e, h=h, pv=pv, pvk=pvk: e.tensor_copy(
                        encT[:, h, :], pv[:, 0:128]),
                        reads=[pvk], writes=[("encT", h)])

                enc_reads = [("encT", i) for i in range(NH)]

                for s in range(S):
                    b.dma("gpsimd", wbig[:, 0:KT, :], qkv_[:, :, :], writes=[("wbig",)])
                    b.dma("gpsimd", kvt[:, :, 0:KVW], kkv[:, :, :], writes=[("kvt",)])
                    a_stage1(s, 0)
                    for it in range(TT):
                        a_proj(s, it)
                        if it + 1 < TT:
                            a_stage1(s, it + 1)
                        a_rope(s, it)

                    b.dma("gpsimd", kvt[:, :, 0:KVW], vkv[:, :, :], writes=[("kvt",)])
                    a_stage1(s, 0)
                    for it in range(TT):
                        slot = it % 2
                        pv = pC0 if it % 2 == 0 else pC1
                        pvk = ("pC0",) if it % 2 == 0 else ("pC1",)
                        for kt in range(KT):
                            b.op("tensor", lambda e, kt=kt, slot=slot, pv=pv: e.matmul(
                                pv[:, 0:KVW], xfm[:, slot, kt, :], kvt[:, kt, 0:KVW],
                                start=(kt == 0), stop=(kt == KT - 1)),
                                reads=[("xfm", slot), ("kvt",)], writes=[pvk])
                        if it + 1 < TT:
                            a_stage1(s, it + 1)
                        b.op("vector", lambda e, it=it, pv=pv, pvk=pvk: e.tensor_copy(
                            v_tm[:, it, 0:KVW], pv[:, 0:KVW]),
                            reads=[pvk], writes=[("v_tm", it)])

                    b.dma("gpsimd", wbig[:, 0:NH, :], okv[:, :, :], writes=[("wbig",)])

                    c2_load(0)
                    for it in range(TT):
                        slot = it % 2
                        xsl = it % 3
                        tkey = ("xt", xsl)
                        b.dma("sync", xt[:, xsl, :], src[s, it * 128:(it + 1) * 128, :],
                              reads=[(src_name, s, it)], writes=[tkey])
                        if it + 1 < TT:
                            c2_load(it + 1)
                        for h in range(NH):
                            c2_stage1(it, h)
                            if h > 0:
                                c2_stage2(it, h - 1)
                            if h > 1:
                                c2_stage2b(it, h - 2)
                        c2_stage2(it, NH - 1)
                        c2_stage2b(it, NH - 2)
                        c2_stage2b(it, NH - 1)
                        for c in range(DC):
                            p, off, pk = PD(c)
                            for kt in range(NH):
                                b.op("tensor", lambda e, c=c, kt=kt, p=p, off=off: e.matmul(
                                    p[:, off:off + 512], encT[:, kt, :],
                                    wbig[:, kt, c * 512:(c + 1) * 512],
                                    start=(kt == 0), stop=(kt == NH - 1)),
                                    reads=[("wbig",)] + enc_reads, writes=[pk])
                        for c in range(DC):
                            p, off, pk = PD(c)
                            b.op("vector", lambda e, c=c, xsl=xsl, p=p, off=off: e.tensor_tensor(
                                u1[:, c * 512:(c + 1) * 512], p[:, off:off + 512],
                                xt[:, xsl, c * 512:(c + 1) * 512], ALU.add),
                                reads=[pk, tkey], writes=[("u1",)])
                        b.dma("scalar", dst[s, it * 128:(it + 1) * 128, :], u1[:, 0:D],
                              reads=[("u1",)], writes=[(dst_name, s, it)])
            b.barrier()

        ffn(x_in, "x", xs1, "xs1", prep=True)
        attn(xs1, "xs1", xs2, "xs2")
        ffn(xs2, "xs2", y_out, "y", prep=False)

        b.finish()
    return nc


def kernel(**inputs):
    cfg = cfg_full()
    n_cores = 8
    per = inputs["L0_attn_x"].shape[0] // n_cores
    nc = build_nc(cfg)
    namemap = {
        "qk": "L0_attn_q_kernel", "kk": "L0_attn_k_kernel", "vk": "L0_attn_v_kernel",
        "ok": "L0_attn_out_kernel", "ing": "L0_ffn_in_gamma", "inb": "L0_ffn_in_beta",
        "aggw": "L0_ffn_agg_w", "aggb": "L0_ffn_agg_b", "png": "L0_ffn_pn_gamma",
        "pnb": "L0_ffn_pn_beta", "d1w": "L0_ffn_d1_w", "d1b": "L0_ffn_d1_b",
        "d2w": "L0_ffn_d2_w", "d2b": "L0_ffn_d2_b", "bcw": "L0_ffn_bc_w",
        "bcb": "L0_ffn_bc_b", "gw": "L0_ffn_gate_w", "gb": "L0_ffn_gate_b",
        "outg": "L0_ffn_out_gamma", "outb": "L0_ffn_out_beta",
    }
    x = np.ascontiguousarray(np.asarray(inputs["L0_attn_x"], dtype=np.float32))
    weights = {k: np.ascontiguousarray(np.asarray(inputs[v], dtype=np.float32))
               for k, v in namemap.items()}
    in_maps = []
    for c in range(n_cores):
        m = {"x": x[c * per:(c + 1) * per]}
        m.update(weights)
        in_maps.append(m)
    res = run_bass_kernel_spmd(nc, in_maps, core_ids=list(range(n_cores)))
    return np.concatenate([np.asarray(r["y"]) for r in res.results], axis=0)
